# version 09
# baseline (speedup 1.0000x reference)
import numpy as np
import concourse.bass as bass
import concourse.mybir as mybir
from concourse.bass_utils import run_bass_kernel_spmd
from concourse.bass_core_rust import IndirectOffsetOnAxis

F32 = mybir.dt.float32
BF16 = mybir.dt.bfloat16
AF = mybir.ActivationFunctionType
ALU = mybir.AluOpType
AX = mybir.AxisListType

D = 1024
H, DH, W = 16, 64, 128
E, F = 8, 4096
CH = 1024
NW = CH // W
DB = D // 128
FB = 16
SCALE = DH ** -0.5
EPS = 1e-5
USE_WCACHE = False
C_CAP = 416
SLOT = 512
XE_ROWS = E * SLOT + SLOT

SBUF_BASE = 16512
SBUF_TOP = 229344


class Sem:
    def __init__(self, nc, name):
        self.h = nc.alloc_semaphore(name)
        self.n = 0


class Rec:
    ENGS = ("t", "v", "s", "g", "y")

    def __init__(self):
        self.ops = {e: [] for e in self.ENGS}

    def emit(self, eng, fn, wait=(), inc=()):
        for sem, val in wait:
            assert val <= sem.n, f"wait on {sem.h.name} for {val} > recorded {sem.n}"
        self.ops[eng].append((fn, tuple(wait), tuple(inc)))
        for sem, k in inc:
            sem.n += k


def build_core_program(nc: bass.Bass, n_chunks: int, stage: int = 99, wstage: int = 99):
    TPC = n_chunks * CH

    xp = nc.dram_tensor("xp", [W + TPC, D], F32, kind="ExternalInput").ap()
    wq_d = nc.dram_tensor("wq", [128, DB, D], F32, kind="ExternalInput").ap()
    wkv_d = nc.dram_tensor("wkv", [128, DB, 2 * D], F32, kind="ExternalInput").ap()
    wo_d = nc.dram_tensor("wo", [128, DB, D], F32, kind="ExternalInput").ap()
    gw_d = nc.dram_tensor("gw", [128, DB, E], F32, kind="ExternalInput").ap()
    w1_d = nc.dram_tensor("w1", [E, 8, 128, DB, 512], F32, kind="ExternalInput").ap()
    w3_d = nc.dram_tensor("w3", [E, 8, 128, DB, 512], F32, kind="ExternalInput").ap()
    w2_d = nc.dram_tensor("w2", [E, 4, 128, 8, D], F32, kind="ExternalInput").ap()
    L128_d = nc.dram_tensor("L128", [128, 128], F32, kind="ExternalInput").ap()
    onescol_d = nc.dram_tensor("onescol", [128, 1], F32, kind="ExternalInput").ap()
    ones1_d = nc.dram_tensor("ones1", [1, 128], F32, kind="ExternalInput").ap()
    iota8_d = nc.dram_tensor("iota8", [128, E], F32, kind="ExternalInput").ap()
    xehbm = nc.dram_tensor("xehbm", [XE_ROWS, D], BF16, kind="Internal").ap()
    w1c = nc.dram_tensor("w1c", [E, 8, 128, DB, 512], BF16, kind="Internal").ap()
    w3c = nc.dram_tensor("w3c", [E, 8, 128, DB, 512], BF16, kind="Internal").ap()
    w2c = nc.dram_tensor("w2c", [E, 4, 128, 8, D], BF16, kind="Internal").ap()
    outhbm = nc.dram_tensor("outhbm", [XE_ROWS, D], F32, kind="Internal").ap()
    bo_bc_d = nc.dram_tensor("bo_bc", [128, D], F32, kind="ExternalInput").ap()
    mask4_d = nc.dram_tensor("mask4", [128, 4 * 256], F32, kind="ExternalInput").ap()
    ident_d = nc.dram_tensor("ident", [128, 128], F32, kind="ExternalInput").ap()
    out_d = nc.dram_tensor("out", [TPC, D], F32, kind="ExternalOutput").ap()

    off = [SBUF_BASE]

    def sb(name, shape, dtype, at=None):
        nbytes = int(np.prod(shape[1:])) * mybir.dt.size(dtype)
        if at is None:
            at = off[0]
            off[0] += (nbytes + 63) & ~63
        t = nc.alloc_sbuf_tensor_at(name, shape, dtype, offset=at)
        assert at + nbytes <= SBUF_TOP, f"{name} overflows sbuf: {at}+{nbytes}"
        return t

    acc = sb("acc", [128, NW, D], F32)
    bo_b = sb("bo_b", [128, D], F32)
    mask4 = sb("mask4s", [128, 4 * 256], F32)
    ident = sb("idents", [128, 128], BF16)
    gwb = sb("gwb", [128, DB, E], BF16)
    wi = sb("wi", [128, NW, E], F32)
    stats = sb("stats", [128, 64], F32)
    st2 = sb("st2", [128, NW], F32)
    ve8 = sb("ve8", [128, NW], F32)
    sd8 = sb("sd8", [128, NW], F32)
    rstd8 = sb("rstd8", [128, NW], F32)
    st2b = sb("st2b", [128, NW], F32)
    ve8b = sb("ve8b", [128, NW], F32)
    sd8b = sb("sd8b", [128, NW], F32)
    rstd8b = sb("rstd8b", [128, NW], F32)
    lgblk = sb("lgblk", [128, 384], F32)
    posf = sb("posf", [128, 64], F32)
    L128s = sb("L128s", [128, 128], F32)
    onescol = sb("onescol", [128, 1], F32)
    ones1s = sb("ones1s", [1, 128], F32)
    iota8s = sb("iota8s", [128, E], F32)
    I32 = mybir.dt.int32
    s1i = sb("s1i", [128, NW], I32)
    s2i = sb("s2i", [128, NW], I32)
    g1i = sb("g1i", [128, NW], I32)
    g2i = sb("g2i", [128, NW], I32)
    w1sel = sb("w1sel", [128, NW], F32)
    w2sel = sb("w2sel", [128, NW], F32)
    ttlrow = sb("ttlrow", [1, NW, E], F32)
    baserow = sb("baserow", [1, NW, E], F32)
    cb = sb("cb", [128, 2], F32)
    shared0 = off[0]

    offA = [shared0]
    offsave = off[0]

    def sba(name, shape, dtype):
        off[0] = offA[0]
        t = sb(name, shape, dtype)
        offA[0] = off[0]
        return t

    xT = sba("xT", [128, DB, W + CH], BF16)
    qT = sba("qT", [128, DB, CH], BF16)
    kT = sba("kT", [128, DB, W + CH], BF16)
    vb = sba("vb", [128, NW + 1, D], BF16)
    wqb = sba("wqb", [128, DB, D], BF16)
    wkvb = sba("wkvb", [128, DB, 2 * D], BF16)
    wob = sba("wob", [128, DB, D], BF16)
    xb = [sba(f"xb{i}", [128, D], BF16) for i in range(2)]
    e4a = sba("e4a", [128, 4 * 256], F32)
    e4b = sba("e4b", [128, 4 * 256], F32)
    p4 = sba("p4", [128, 4, 256], BF16)
    pTs = sba("pTs", [128, 4, 2, 128], BF16)
    oT = sba("oT", [128, DB, 128], BF16)
    ob = sba("ob", [128, 256], BF16)
    xf = [sba(f"xf{i}", [128, D], F32) for i in range(2)]
    xg = xf
    a_end = offA[0]

    offB = [shared0]

    def sbb_(name, shape, dtype):
        off[0] = offB[0]
        t = sb(name, shape, dtype)
        offB[0] = off[0]
        return t

    x1T = sbb_("x1T", [128, DB, CH], BF16)
    x1btm = sbb_("x1btm", [128, NW, D], BF16)
    xeT = [sbb_(f"xeT{i}", [128, DB, SLOT], BF16) for i in range(2)]
    hbuf = sbb_("hbuf", [128, 32, SLOT], BF16)
    w2b = [sbb_(f"w2b{i}", [128, 8, D], BF16) for i in range(2)]
    w1b = [sbb_(f"w1b{i}", [128, DB, 512], BF16) for i in range(2)]
    w3b = [sbb_(f"w3b{i}", [128, DB, 512], BF16) for i in range(2)]
    sg = [sbb_(f"sg{i}", [128, 512], F32) for i in range(2)]
    ostg = [sbb_(f"ostg{i}", [128, D], F32) for i in range(2)]
    gstage = ostg
    outf = ostg
    b_end = offB[0]
    off[0] = offsave
    assert max(a_end, b_end) <= SBUF_TOP, (a_end, b_end)

    PA = nc.alloc_psum_tensor("PA", [128, 1024], F32)
    PB = nc.alloc_psum_tensor("PB", [128, 1024], F32)
    PC = nc.alloc_psum_tensor("PC", [128, 512], F32)
    PD = nc.alloc_psum_tensor("PD", [128, 512], F32)
    PE2 = nc.alloc_psum_tensor("PE2", [128, 512], F32)
    PF = nc.alloc_psum_tensor("PF", [128, 512], F32)
    PCb = PC[:].bitcast(BF16)
    PDb = PD[:].bitcast(BF16)
    g1p = [PC, PE2]
    g3p = [PD, PF]
    outp = [PA, PB]

    pe = Sem(nc, "pe")
    ve = Sem(nc, "ve")
    se = Sem(nc, "se")
    sInit = Sem(nc, "sInit")
    swA = Sem(nc, "swA")
    sW13 = [Sem(nc, "sW13a"), Sem(nc, "sW13b")]
    sW2q = [Sem(nc, "sW2a"), Sem(nc, "sW2b")]
    sW13h = [Sem(nc, "sW13ha"), Sem(nc, "sW13hb")]
    sWC = [Sem(nc, "sWCa"), Sem(nc, "sWCb")]
    sWC2 = [Sem(nc, "sWC2a"), Sem(nc, "sWC2b")]
    sSC = Sem(nc, "sSC")
    sXT = [Sem(nc, "sXTa"), Sem(nc, "sXTb")]
    sOB = [Sem(nc, "sOBa"), Sem(nc, "sOBb")]
    sGB = [Sem(nc, "sGBa"), Sem(nc, "sGBb")]
    sMZ = Sem(nc, "sMZ")
    sXF = [Sem(nc, "sXFa"), Sem(nc, "sXFb")]
    sXG = sXF
    sXO = [Sem(nc, "sXOa"), Sem(nc, "sXOb")]

    r = Rec()

    r.emit("g", lambda g: g.dma_start(ident[:], ident_d), inc=[(sInit, 16)])
    r.emit("g", lambda g: g.dma_start(mask4[:], mask4_d), inc=[(sInit, 16)])
    r.emit("g", lambda g: g.dma_start(bo_b[:], bo_bc_d), inc=[(sInit, 16)])
    r.emit("g", lambda g: g.dma_start(gwb[:], gw_d), inc=[(sInit, 16)])
    r.emit("g", lambda g: g.dma_start(L128s[:], L128_d), inc=[(sInit, 16)])
    r.emit("g", lambda g: g.dma_start(onescol[:], onescol_d), inc=[(sInit, 16)])
    r.emit("g", lambda g: g.dma_start(ones1s[:], ones1_d), inc=[(sInit, 16)])
    r.emit("g", lambda g: g.dma_start(iota8s[:], iota8_d), inc=[(sInit, 16)])
    sInitN = sInit.n
    r.emit("v", lambda v: v.memset(cb[:, 0:1], EPS), inc=[(ve, 1)])
    r.emit("v", lambda v: v.memset(cb[:, 1:2], SCALE), inc=[(ve, 1)])
    ve_cb = ve.n
    r.emit("v", lambda v: v.memset(x1btm[:, 0, :], 0.0), inc=[(ve, 1)])
    r.emit("v", lambda v: v.memset(ostg[0][:], 0.0), inc=[(ve, 1)])
    ve_mz = ve.n
    xer = xehbm.rearrange("(n p) d -> n p d", p=128)
    for n in range(XE_ROWS // 128):
        r.emit("g", lambda g, n=n: g.dma_start(xer[n], x1btm[:, 0, :]),
               wait=[(ve, ve_mz)], inc=[(sMZ, 16)])
    ohr = outhbm.rearrange("(n p) d -> n p d", p=128)
    for n in range(E * SLOT // 128, XE_ROWS // 128):
        r.emit("g", lambda g, n=n: g.dma_start(ohr[n], ostg[0][:]),
               wait=[(ve, ve_mz)], inc=[(sMZ, 16)])
    sMZ_full = sMZ.n

    ve_xbcopy = [0, 0]
    pe_xtr = [0, 0]
    ve_trevac = [0, 0]
    se_qkvevac = [0, 0]
    ve_maskadd = 0
    ve_pTevac = 0
    ve_oevac = 0
    ve_oTevac = 0
    ve_epiTT1 = 0
    ve_TT2 = [0, 0]
    sXG_cnt = [0, 0]
    sXF_cnt = [0, 0]
    sXO_cnt = [0, 0]
    ve_outf = [0, 0]
    pe_endB = 0
    ve_endB = 0
    pe_w13use = [0, 0]
    pe_w2q = [0, 0]
    ve_hmul = [0, 0]
    se_silu = [0, 0]
    ve_lgcopy = 0
    ve_ttl = 0
    ve_posf = 0
    ve_outev = [0, 0, 0, 0]
    sOB_cnt = [0, 0]
    sGB_cnt = [0, 0]
    ve_gstt = [0, 0]
    pe_hlast = [0, 0]
    sXT_cnt = [0, 0]
    ve_x1bT = 0
    pe_x1tr = 0
    pe_lastout = 0

    for c in range(n_chunks):
        gate = [(pe, pe_endB), (ve, ve_endB),
                (sXO[0], sXO_cnt[0]), (sXO[1], sXO_cnt[1])]
        r.emit("g", lambda g: g.dma_start(wqb[:], wq_d),
               wait=gate, inc=[(swA, 16)])
        r.emit("g", lambda g: g.dma_start(wkvb[:], wkv_d), inc=[(swA, 16)])
        r.emit("g", lambda g: g.dma_start(wob[:], wo_d), inc=[(swA, 16)])
        swA_full = swA.n

        for i in range(NW + 1):
            par = i % 2
            row0 = c * CH + i * 128
            r.emit("y", lambda y, r0=row0, p=par: y.dma_start(
                xf[p][:], xp[r0:r0 + 128, :]),
                wait=[(ve, ve_xbcopy[par]), (ve, ve_TT2[par]), (pe, pe_endB)],
                inc=[(sXF[par], 16)])
            sXF_cnt[par] = sXF[par].n
            r.emit("v", lambda v, p=par: v.tensor_copy(xb[p][:], xf[p][:]),
                   wait=[(sXF[par], sXF_cnt[par]), (pe, pe_xtr[par])], inc=[(ve, 1)])
            ve_xbcopy[par] = ve.n
            scr = PCb if par == 0 else PDb

            def xtr(t, p=par, s=scr):
                for db in range(DB):
                    inst = t.transpose(s[:, db * 128:(db + 1) * 128],
                                       xb[p][:, db * 128:(db + 1) * 128], ident[:])
                return inst
            r.emit("t", xtr, wait=[(ve, ve_xbcopy[par]), (ve, ve_trevac[par]),
                                   (sInit, sInitN)], inc=[(pe, 1)])
            pe_xtr[par] = pe.n
            r.emit("v", lambda v, i=i, s=scr: v.tensor_copy(
                xT[:, :, i * 128:(i + 1) * 128],
                s.rearrange("p (db t) -> p db t", db=DB)),
                wait=[(pe, pe.n), (sMZ, sMZ_full)], inc=[(ve, 1)])
            ve_trevac[par] = ve.n
        ve_xT = ve.n

        if stage <= 1:
            continue
        def proj(dst_evac, lhs_w, lhs_col0, rhs_src, rhs_off, n_cols, psum_par,
                 scale=None, tok_lhs=False):
            P = [PC, PD][psum_par]

            def mm(t):
                for db in range(DB):
                    if tok_lhs:
                        lhsT = xT[:, db, lhs_col0:lhs_col0 + 128]
                        rhs = rhs_src[:, db, rhs_off:rhs_off + n_cols]
                    else:
                        lhsT = lhs_w[:, db, lhs_col0:lhs_col0 + 128]
                        rhs = rhs_src[:, db, rhs_off:rhs_off + n_cols]
                    inst = t.matmul(P[:, 0:n_cols], lhsT, rhs,
                                    start=(db == 0), stop=(db == DB - 1))
                return inst
            r.emit("t", mm, wait=[(ve, ve_xT), (swA, swA_full),
                                  (se, se_qkvevac[psum_par])], inc=[(pe, 1)])

            def evac(s, P=P, n=n_cols):
                if scale is None:
                    return s.copy(dst_evac, P[:, 0:n])
                return s.activation(dst_evac, P[:, 0:n], AF.Copy, scale=cb[:, 1:2])
            r.emit("s", evac, wait=[(pe, pe.n), (ve, ve_cb)], inc=[(se, 1)])
            se_qkvevac[psum_par] = se.n

        pp = 0
        for ib in range(DB):
            for th in range(2):
                proj(qT[:, ib, th * 512:(th + 1) * 512], wqb, ib * 128,
                     xT, W + th * 512, 512, pp % 2, scale=SCALE)
                pp += 1
        for ib in range(DB):
            for (o0, nc_) in ((0, 512), (512, 512), (1024, 128)):
                proj(kT[:, ib, o0:o0 + nc_], wkvb, ib * 128,
                     xT, o0, nc_, pp % 2)
                pp += 1
        for tb in range(NW + 1):
            for nh in range(2):
                proj(vb[:, tb, nh * 512:(nh + 1) * 512], None, tb * 128,
                     wkvb, D + nh * 512, 512, pp % 2, tok_lhs=True)
                pp += 1
        se_qkv = se.n

        if stage <= 2:
            continue
        for w in range(NW):
            for hb in range(4):
                def sims(t, w=w, hb=hb):
                    for s in range(4):
                        h = hb * 4 + [0, 2, 1, 3][s]
                        hp, hw = h % 2, h // 2
                        lhsT = qT[hp * 64:(hp + 1) * 64, hw, w * 128:(w + 1) * 128]
                        rhs = kT[hp * 64:(hp + 1) * 64, hw, w * 128:w * 128 + 256]
                        inst = t.matmul(PA[:, s * 256:(s + 1) * 256], lhsT, rhs,
                                        start=True, stop=True)
                    return inst
                r.emit("t", sims, wait=[(se, se_qkv), (ve, ve_maskadd)],
                       inc=[(pe, 1)])
                if wstage <= 0:
                    continue
                r.emit("v", lambda v: v.tensor_add(e4b[:], PA[:], mask4[:]),
                       wait=[(pe, pe.n), (sInit, sInitN), (sMZ, sMZ_full)],
                       inc=[(ve, 1)])
                ve_maskadd = ve.n
                if wstage <= -1:
                    pass
                r.emit("v", lambda v: v.reduce_max(
                    stats[:, 0:4], e4b[:].rearrange("p (s x) -> p s x", s=4),
                    axis=AX.X, negate=True), wait=[(ve, ve.n)], inc=[(ve, 1)])

                def exps(s):
                    for hh in range(4):
                        inst = s.activation(e4a[:, hh * 256:(hh + 1) * 256],
                                            e4b[:, hh * 256:(hh + 1) * 256],
                                            AF.Exp, bias=stats[:, hh:hh + 1],
                                            accum_out=stats[:, 8 + hh:9 + hh])
                    return inst
                if wstage <= 1:
                    continue
                r.emit("s", exps, wait=[(ve, ve.n)], inc=[(se, 1)])
                r.emit("v", lambda v: v.reciprocal(stats[:, 12:16], stats[:, 8:12]),
                       wait=[(se, se.n)], inc=[(ve, 1)])

                def pnorm(v):
                    for hh in range(4):
                        inst = v.tensor_scalar_mul(
                            p4[:, hh, :], e4a[:, hh * 256:(hh + 1) * 256],
                            stats[:, 12 + hh:13 + hh])
                    return inst
                r.emit("v", pnorm, wait=[(ve, ve.n)], inc=[(ve, 1)])
                ve_p4 = ve.n

                def ptr(t):
                    for hh in range(4):
                        for jh in range(2):
                            inst = t.transpose(
                                PCb[:, (hh * 2 + jh) * 128:(hh * 2 + jh + 1) * 128],
                                p4[:, hh, jh * 128:(jh + 1) * 128], ident[:])
                    return inst
                if wstage <= 2:
                    continue
                r.emit("t", ptr, wait=[(ve, ve_p4), (ve, ve_pTevac)], inc=[(pe, 1)])
                r.emit("v", lambda v: v.tensor_copy(
                    pTs[:].rearrange("p a b t -> p (a b t)"), PCb[:]),
                    wait=[(pe, pe.n)], inc=[(ve, 1)])
                ve_pTevac = ve.n

                if wstage <= 3:
                    continue

                def omm(t, w=w, hb=hb):
                    for s in range(4):
                        h = hb * 4 + [0, 2, 1, 3][s]
                        for jh in range(2):
                            inst = t.matmul(
                                PE2[:, s * 64:(s + 1) * 64],
                                pTs[:, s, jh, :],
                                vb[:, w + jh, h * 64:(h + 1) * 64],
                                start=(jh == 0), stop=(jh == 1))
                    return inst
                r.emit("t", omm, wait=[(ve, ve_pTevac), (ve, ve_oevac)],
                       inc=[(pe, 1)])
                r.emit("v", lambda v: v.tensor_copy(ob[:], PE2[:, 0:256]),
                       wait=[(pe, pe.n)], inc=[(ve, 1)])
                ve_oevac = ve.n

                def otr(t, hb=hb):
                    for k in range(2):
                        inst = t.transpose(
                            PDb[:, (hb * 2 + k) * 128:(hb * 2 + k + 1) * 128],
                            ob[:, k * 128:(k + 1) * 128], ident[:])
                    return inst
                r.emit("t", otr, wait=[(ve, ve_oevac), (ve, ve_oTevac)],
                       inc=[(pe, 1)])
            if wstage <= 4:
                continue
            r.emit("v", lambda v: v.tensor_copy(
                oT[:].rearrange("p a t -> p (a t)"), PDb[:]),
                wait=[(pe, pe.n)], inc=[(ve, 1)])
            ve_oTevac = ve.n

            def womm(t):
                for nh in range(2):
                    for ib in range(DB):
                        inst = t.matmul(PB[:, nh * 512:(nh + 1) * 512],
                                        oT[:, ib, :],
                                        wob[:, ib, nh * 512:(nh + 1) * 512],
                                        start=(ib == 0), stop=(ib == DB - 1))
                return inst
            r.emit("t", womm, wait=[(ve, ve_oTevac), (swA, swA_full),
                                    (ve, ve_epiTT1)], inc=[(pe, 1)])
            if wstage <= 5:
                continue
            par = w % 2
            row0 = W + c * CH + w * 128
            r.emit("y", lambda y, r0=row0, p=par: y.dma_start(
                xg[p][:], xp[r0:r0 + 128, :]),
                wait=[(ve, ve_TT2[par]), (ve, ve_xT)], inc=[(sXG[par], 16)])
            sXG_cnt[par] = sXG[par].n
            r.emit("v", lambda v, w=w: v.tensor_add(acc[:, w, :], PB[:], bo_b[:]),
                   wait=[(pe, pe.n), (sInit, sInitN)], inc=[(ve, 1)])
            ve_epiTT1 = ve.n
            r.emit("v", lambda v, w=w, p=par: v.tensor_add(
                acc[:, w, :], acc[:, w, :], xg[p][:]),
                wait=[(sXG[par], sXG_cnt[par]), (ve, ve.n)], inc=[(ve, 1)])
            ve_TT2[par] = ve.n
            r.emit("v", lambda v, w=w: v.reduce_sum(
                stats[:, 16:17], acc[:, w, :], axis=AX.X, negate=True),
                wait=[(ve, ve.n)], inc=[(ve, 1)])
            r.emit("v", lambda v, w=w: v.tensor_scalar_mul(
                st2[:, w:w + 1], stats[:, 16:17], 1.0 / D),
                wait=[(ve, ve.n)], inc=[(ve, 1)])
            r.emit("v", lambda v, w=w: v.scalar_tensor_tensor(
                e4a[:], acc[:, w, :], 1.0, acc[:, w, :],
                op0=ALU.mult, op1=ALU.mult, accum_out=stats[:, 18:19]),
                wait=[(ve, ve.n)], inc=[(ve, 1)])
            r.emit("v", lambda v, w=w: v.tensor_mul(
                stats[:, 19:20], st2[:, w:w + 1], st2[:, w:w + 1]),
                wait=[(ve, ve.n)], inc=[(ve, 1)])
            r.emit("v", lambda v, w=w: v.tensor_scalar(
                ve8[:, w:w + 1], stats[:, 18:19], 1.0 / D, stats[:, 19:20],
                op0=ALU.mult, op1=ALU.subtract),
                wait=[(ve, ve.n)], inc=[(ve, 1)])
        if wstage <= 5:
            continue
        r.emit("s", lambda s: s.activation(sd8[:], ve8[:], AF.Sqrt, bias=cb[:, 0:1]),
               wait=[(ve, ve.n)], inc=[(se, 1)])
        r.emit("v", lambda v: v.reciprocal(rstd8[:], sd8[:]),
               wait=[(se, se.n)], inc=[(ve, 1)])
        for w in range(NW):
            r.emit("v", lambda v, w=w: v.tensor_scalar(
                acc[:, w, :], acc[:, w, :], st2[:, w:w + 1], rstd8[:, w:w + 1],
                op0=ALU.add, op1=ALU.mult),
                wait=[(ve, ve.n)], inc=[(ve, 1)])
        ve_x1 = ve.n

        if stage <= 3:
            continue
        pe_womm = pe.n
        hoist13 = {}
        hoistw2 = {}
        ve_hoist = ve.n
        w1s, w3s, w2s = ((w1_d, w3_d, w2_d) if (c == 0 or not USE_WCACHE)
                         else (w1c, w3c, w2c))

        def wb13(e, f8):
            if c != 0 or not USE_WCACHE:
                return
            par = f8 % 2
            r.emit("y", lambda y, e=e, f8=f8, par=par: y.dma_start(
                w1c[e, f8], w1b[par][:]),
                wait=[(sW13[par], sW13[par].n)], inc=[(sWC[par], 16)])
            r.emit("y", lambda y, e=e, f8=f8, par=par: y.dma_start(
                w3c[e, f8], w3b[par][:]), inc=[(sWC[par], 16)])

        def wb2(e, q):
            if c != 0 or not USE_WCACHE:
                return
            par = q % 2
            r.emit("y", lambda y, e=e, q=q, par=par: y.dma_start(
                w2c[e, q], w2b[par][:]),
                wait=[(sW2q[par], sW2q[par].n)], inc=[(sWC2[par], 16)])

        for f8 in range(2):
            r.emit("g", lambda g, f8=f8, w1s=w1s: g.dma_start(
                w1b[f8 % 2][:], w1s[0, f8]),
                wait=[(pe, pe_womm), (pe, pe_w13use[f8 % 2]), (ve, ve_hoist),
                      (sWC[f8 % 2], sWC[f8 % 2].n)],
                inc=[(sW13[f8 % 2], 16)])
            r.emit("g", lambda g, f8=f8, w3s=w3s: g.dma_start(
                w3b[f8 % 2][:], w3s[0, f8]),
                inc=[(sW13[f8 % 2], 16)])
            hoist13[f8] = sW13[f8 % 2].n
            wb13(0, f8)
        for q in range(2):
            r.emit("g", lambda g, q=q, w2s=w2s: g.dma_start(
                w2b[q % 2][:], w2s[0, q]),
                wait=[(pe, pe_womm), (pe, pe_w2q[q % 2]), (ve, ve_hoist),
                      (sWC2[q % 2], sWC2[q % 2].n)],
                inc=[(sW2q[q % 2], 16)])
            hoistw2[q] = sW2q[q % 2].n
            wb2(0, q)
        for tt in range(NW):
            par = tt % 2
            scr = PCb if par == 0 else PDb
            r.emit("v", lambda v, tt=tt: v.tensor_copy(x1btm[:, tt, :], acc[:, tt, :]),
                   wait=[(ve, ve_x1), (sSC, sSC.n)], inc=[(ve, 1)])
            ve_x1bT = ve.n

            def x1tr(t, s=scr, tt=tt):
                for db in range(DB):
                    inst = t.transpose(s[:, db * 128:(db + 1) * 128],
                                       x1btm[:, tt, db * 128:(db + 1) * 128], ident[:])
                return inst
            r.emit("t", x1tr, wait=[(ve, ve_x1bT), (ve, ve_trevac[par])],
                   inc=[(pe, 1)])
            pe_x1tr = pe.n
            r.emit("v", lambda v, tt=tt, s=scr: v.tensor_copy(
                x1T[:, :, tt * 128:(tt + 1) * 128],
                s.rearrange("p (db t) -> p db t", db=DB)),
                wait=[(pe, pe.n)], inc=[(ve, 1)])
            ve_trevac[par] = ve.n
        ve_x1T = ve.n

        import dataclasses as _dc

        def bc8(ap2d):
            return _dc.replace(ap2d, ap=[list(ap2d.ap[0]), list(ap2d.ap[1]), [0, 8]])

        def seg(ap2d):
            return ap2d.rearrange("p (s x) -> p s x", s=8)

        for tt in range(NW):
            def gmm(t, tt=tt):
                for db in range(DB):
                    inst = t.matmul(PE2[:, tt * 8:(tt + 1) * 8],
                                    x1T[:, db, tt * 128:(tt + 1) * 128],
                                    gwb[:, db, :], start=(db == 0), stop=(db == DB - 1))
                return inst
            r.emit("t", gmm, wait=[(ve, ve_x1T), (sInit, sInitN), (ve, ve_lgcopy)],
                   inc=[(pe, 1)])
        r.emit("v", lambda v: v.tensor_copy(lgblk[:, 0:64], PE2[:, 0:64]),
               wait=[(pe, pe.n)], inc=[(ve, 1)])
        ve_lgcopy = ve.n
        L = lgblk

        def chain(fn, eng="v"):
            r.emit(eng, fn, wait=[(ve, ve.n)] if eng == "v" else [(ve, ve.n)],
                   inc=[(ve, 1) if eng == "v" else (se, 1)])
        r.emit("s", lambda s: s.activation(L[:, 64:128], L[:, 0:64], AF.Exp),
               wait=[(ve, ve.n)], inc=[(se, 1)])
        r.emit("v", lambda v: v.reduce_max(L[:, 256:264], seg(L[:, 64:128]),
                                           axis=AX.X, negate=True),
               wait=[(se, se.n)], inc=[(ve, 1)])
        chain(lambda v: v.tensor_scalar_mul(L[:, 264:272], L[:, 256:264], -1.0))
        chain(lambda v: v.tensor_tensor(seg(L[:, 128:192]), seg(L[:, 64:128]),
                                        bc8(L[:, 264:272]), op=ALU.is_equal))
        chain(lambda v: v.scalar_tensor_tensor(L[:, 192:256], L[:, 128:192], -1e30,
                                               L[:, 64:128], op0=ALU.mult, op1=ALU.add))
        chain(lambda v: v.reduce_max(L[:, 272:280], seg(L[:, 192:256]),
                                     axis=AX.X, negate=True))
        chain(lambda v: v.tensor_scalar_mul(L[:, 280:288], L[:, 272:280], -1.0))
        chain(lambda v: v.tensor_tensor(seg(L[:, 192:256]), seg(L[:, 64:128]),
                                        bc8(L[:, 280:288]), op=ALU.is_ge))
        chain(lambda v: v.tensor_add(L[:, 288:296], L[:, 264:272], L[:, 280:288]))
        chain(lambda v: v.reciprocal(L[:, 296:304], L[:, 288:296]))
        chain(lambda v: v.tensor_mul(w1sel[:].rearrange("p a -> p a"), L[:, 264:272],
                                     L[:, 296:304]))
        chain(lambda v: v.tensor_mul(w2sel[:].rearrange("p a -> p a"), L[:, 280:288],
                                     L[:, 296:304]))
        ve_sel = ve.n
        for tt in range(NW):
            r.emit("t", lambda t, tt=tt: t.matmul(
                PF[0:1, tt * 8:(tt + 1) * 8], onescol[:],
                L[:, 192 + tt * 8:192 + (tt + 1) * 8], start=True, stop=True),
                wait=[(ve, ve_sel), (ve, ve_ttl)], inc=[(pe, 1)])
        r.emit("v", lambda v: v.tensor_copy(ttlrow[0:1, :, :].rearrange(
            "p a b -> p (a b)"), PF[0:1, 0:64]), wait=[(pe, pe.n)], inc=[(ve, 1)])
        ve_ttl = ve.n
        r.emit("v", lambda v: v.memset(baserow[0:1, 0, :], 0.0),
               wait=[(ve, ve.n)], inc=[(ve, 1)])
        for tt in range(1, NW):
            chain(lambda v, tt=tt: v.tensor_add(
                baserow[0:1, tt, :], baserow[0:1, tt - 1, :],
                ttlrow[0:1, tt - 1, :]))
        ve_base = ve.n
        for tt in range(NW):
            def posmm(t, tt=tt):
                t.matmul(PC[:, tt * 8:(tt + 1) * 8], L128s[:],
                         L[:, 192 + tt * 8:192 + (tt + 1) * 8],
                         start=True, stop=False)
                return t.matmul(PC[:, tt * 8:(tt + 1) * 8], ones1s[:],
                                baserow[0:1, tt, :], start=False, stop=True)
            r.emit("t", posmm, wait=[(ve, ve_base), (ve, ve_posf)], inc=[(pe, 1)])
        r.emit("v", lambda v: v.tensor_copy(posf[:], PC[:, 0:64]),
               wait=[(pe, pe.n)], inc=[(ve, 1)])
        ve_posf = ve.n
        for mcol, gi, si, wsl in ((264, g1i, s1i, w1sel), (280, g2i, s2i, w2sel)):
            chain(lambda v, m=mcol: v.tensor_tensor(
                seg(L[:, 128:192]), seg(L[:, 64:128]), bc8(L[:, m:m + 8]),
                op=ALU.is_equal))
            chain(lambda v: v.tensor_tensor(
                seg(L[:, 320:384]), seg(L[:, 128:192]),
                _dc.replace(iota8s[:], ap=[list(iota8s[:].ap[0]), [0, 8], [1, 8]]),
                op=ALU.mult))
            chain(lambda v: v.reduce_sum(L[:, 304:312], seg(L[:, 320:384]),
                                         axis=AX.X))
            chain(lambda v: v.tensor_mul(L[:, 320:384], L[:, 128:192], posf[:]))
            chain(lambda v: v.reduce_sum(L[:, 312:320], seg(L[:, 320:384]),
                                         axis=AX.X))
            chain(lambda v: v.tensor_scalar_mul(L[:, 304:312], L[:, 304:312],
                                                float(SLOT)))
            chain(lambda v: v.tensor_add(L[:, 304:312], L[:, 304:312],
                                         L[:, 312:320]))
            chain(lambda v, gi=gi: v.tensor_copy(gi[:], L[:, 304:312]))
            chain(lambda v: v.tensor_single_scalar(L[:, 328:336], L[:, 312:320],
                                                   float(C_CAP), ALU.is_lt))
            chain(lambda v, wsl=wsl: v.tensor_mul(wsl[:], wsl[:], L[:, 328:336]))
            chain(lambda v: v.tensor_scalar(L[:, 336:344], L[:, 328:336], -1.0, 1.0,
                                            op0=ALU.mult, op1=ALU.add))
            chain(lambda v: v.tensor_mul(L[:, 344:352], L[:, 304:312],
                                         L[:, 328:336]))
            chain(lambda v: v.scalar_tensor_tensor(
                L[:, 304:312], L[:, 336:344], float(XE_ROWS - 1), L[:, 344:352],
                op0=ALU.mult, op1=ALU.add))
            chain(lambda v, si=si: v.tensor_copy(si[:], L[:, 304:312]))
        ve_sidx = ve.n
        for tt in range(NW):
            for si in (s1i, s2i):
                wsc = [(ve, ve_sidx)]
                if tt == 0 and si is s1i:
                    wsc += [(sMZ, sMZ_full), (sXT[0], sXT_cnt[0]),
                            (sXT[1], sXT_cnt[1])]
                r.emit("g", lambda g, si=si, tt=tt: g.indirect_dma_start(
                    xehbm, IndirectOffsetOnAxis(si[:, tt:tt + 1], 0),
                    x1btm[:, tt, :], None, oob_is_err=False),
                    wait=wsc, inc=[(sSC, 16)])
        ve_gate = ve.n

        if stage <= 4:
            continue
        sSC_all = sSC.n
        for e in range(E):
            epar = e % 2
            for db in range(DB):
                r.emit("s", lambda s, e=e, db=db, p=epar: s.dma_start(
                    xeT[p][:, db, :],
                    xehbm[e * SLOT:(e + 1) * SLOT, db * 128:(db + 1) * 128],
                    transpose=True),
                    wait=[(sSC, sSC_all), (pe, pe_hlast[epar])] if db == 0 else [],
                    inc=[(sXT[epar], 16)])
            sXT_cnt[epar] = sXT[epar].n
            for f8 in range(8):
                par = f8 % 2

                def wload(wd, dst, p=par, e=e, f8=f8):
                    def go(g, wd=wd, dst=dst, p=p, e=e, f8=f8):
                        return g.dma_start(dst[p][:], wd[e, f8])
                    return go
                if e == 0 and f8 < 2:
                    s13_full = hoist13[f8]
                    s13_half = (sW13[par], s13_full)
                else:
                    def wloadh(wd, dst, half, p=par, e=e, f8=f8):
                        def go(g, wd=wd, dst=dst, half=half, p=p, e=e, f8=f8):
                            return g.dma_start(dst[p][:, half * 4:(half + 1) * 4, :],
                                               wd[e, f8, :, half * 4:(half + 1) * 4, :])
                        return go
                    r.emit("g", wloadh(w1s, w1b, 0),
                           wait=[(pe, pe_w13use[par]), (pe, pe_womm),
                                 (sWC[par], sWC[par].n)],
                           inc=[(sW13h[par], 16)])
                    r.emit("g", wloadh(w3s, w3b, 0), inc=[(sW13h[par], 16)])
                    s13_half = (sW13h[par], sW13h[par].n)
                    r.emit("g", wloadh(w1s, w1b, 1), inc=[(sW13[par], 16)])
                    r.emit("g", wloadh(w3s, w3b, 1), inc=[(sW13[par], 16)])
                    s13_full = sW13[par].n
                    wb13(e, f8)
                for fb in range(4):
                    gpar = fb % 2

                    def hmm(t, wb, P, dbs, fb=fb, p=par, ep=epar):
                        for db in dbs:
                            inst = t.matmul(
                                P[:], wb[p][:, db, fb * 128:(fb + 1) * 128],
                                xeT[ep][:, db, :],
                                start=(db == 0), stop=(db == DB - 1))
                        return inst
                    r.emit("t", lambda t, f=hmm, g=gpar: f(t, w1b, g1p[g], range(4)),
                           wait=[s13_half, (sXT[epar], sXT_cnt[epar]),
                                 (se, se_silu[gpar]), (ve, ve_hmul[gpar]),
                                 (ve, ve_outev[2]), (ve, ve_outev[3])],
                           inc=[(pe, 1)])
                    r.emit("t", lambda t, f=hmm, g=gpar: f(t, w1b, g1p[g], range(4, 8)),
                           wait=[(sW13[par], s13_full)], inc=[(pe, 1)])
                    pe_g1 = pe.n
                    r.emit("t", lambda t, f=hmm, g=gpar: f(t, w3b, g3p[g], range(8)),
                           inc=[(pe, 1)])
                    pe_g3 = pe.n
                    r.emit("s", lambda s, g=gpar: s.activation(
                        sg[g][:], g1p[g][:], AF.Silu),
                        wait=[(pe, pe_g1)], inc=[(se, 1)])
                    se_silu[gpar] = se.n
                    fbi = f8 * 4 + fb
                    r.emit("v", lambda v, g=gpar, fbi=fbi: v.tensor_mul(
                        hbuf[:, fbi, :], sg[g][:], g3p[g][:]),
                        wait=[(se, se_silu[gpar]), (pe, pe_g3)],
                        inc=[(ve, 1)])
                    ve_hmul[gpar] = ve.n
                pe_w13use[par] = pe.n
            pe_hlast[epar] = pe.n
            ve_hdone = ve.n
            def w2load(q, e=e, w2s=w2s):
                return lambda g, e=e, q=q, w2s=w2s: g.dma_start(
                    w2b[q % 2][:], w2s[e, q])
            if e == 0:
                sW2_cnt = [hoistw2[0], hoistw2[1]]
            else:
                for q in range(2):
                    r.emit("g", w2load(q),
                           wait=[(pe, pe_w2q[q % 2]), (pe, pe_womm),
                                 (sWC2[q % 2], sWC2[q % 2].n)],
                           inc=[(sW2q[q % 2], 16)])
                sW2_cnt = [sW2q[0].n, sW2q[1].n]
                wb2(e, 0)
                wb2(e, 1)

            def ounit(st, nh):
                if st == 0:
                    return PA[:, nh * 512:(nh + 1) * 512]
                if st == 1:
                    return PB[:, nh * 512:(nh + 1) * 512]
                if st == 2:
                    return (PC, PD)[nh][:]
                return (PE2, PF)[nh][:]

            for q in range(4):
                def outmm(t, q=q, sts=None):
                    for st in sts:
                        for nh in range(2):
                            for fb in range(8):
                                inst = t.matmul(
                                    ounit(st, nh),
                                    hbuf[:, q * 8 + fb, st * 128:(st + 1) * 128],
                                    w2b[q % 2][:, fb, nh * 512:(nh + 1) * 512],
                                    start=(q == 0 and fb == 0),
                                    stop=(q == 3 and fb == 7))
                    return inst
                wts = [(sW2q[q % 2], sW2_cnt[q % 2])]
                if q == 0:
                    wts += [(ve, ve_hdone), (se, se_silu[0]), (se, se_silu[1]),
                            (ve, ve_hmul[0]), (ve, ve_hmul[1]),
                            (ve, ve_outev[0]), (ve, ve_outev[1]),
                            (ve, ve_outev[2]), (ve, ve_outev[3])]
                if q < 3:
                    r.emit("t", lambda t, f=outmm, q=q: f(t, q, (0, 1, 2, 3)),
                           wait=wts, inc=[(pe, 1)])
                else:
                    r.emit("t", lambda t, f=outmm, q=q: f(t, q, (2, 3)),
                           wait=wts, inc=[(pe, 1)])
                    pe_out_a = pe.n
                    r.emit("t", lambda t, f=outmm, q=q: f(t, q, (0, 1)),
                           inc=[(pe, 1)])
                pe_w2q[q % 2] = pe.n
                if q + 2 < 4:
                    r.emit("g", w2load(q + 2),
                           wait=[(pe, pe_w2q[q % 2]),
                                 (sWC2[q % 2], sWC2[q % 2].n)],
                           inc=[(sW2q[q % 2], 16)])
                    sW2_cnt[q % 2] = sW2q[q % 2].n
                    wb2(e, q + 2)
            pe_out = pe.n
            for st in (2, 3, 0, 1):
                spar = st % 2
                gatep = pe_out_a if st >= 2 else pe_out
                for nh in range(2):
                    r.emit("v", lambda v, st=st, nh=nh, sp=spar: v.tensor_copy(
                        ostg[sp][:, nh * 512:(nh + 1) * 512], ounit(st, nh)),
                        wait=[(pe, gatep), (sOB[spar], sOB_cnt[spar]),
                              (sGB[spar], sGB_cnt[spar]),
                              (sXO[spar], sXO_cnt[spar])],
                        inc=[(ve, 1)])
                ve_outev[st] = ve.n
                r.emit("y", lambda y, e=e, st=st, sp=spar: y.dma_start(
                    outhbm[e * SLOT + st * 128:e * SLOT + (st + 1) * 128, :],
                    ostg[sp][:]),
                    wait=[(ve, ve_outev[st])], inc=[(sOB[spar], 16)])
                sOB_cnt[spar] = sOB[spar].n
        for tt in range(NW):
            for gi, wsl, gpar in ((g1i, w1sel, 0), (g2i, w2sel, 1)):
                w = [(sOB[0], sOB_cnt[0]), (sOB[1], sOB_cnt[1])] if tt == 0 else []
                r.emit("g", lambda g, gi=gi, tt=tt, p=gpar: g.indirect_dma_start(
                    gstage[p][:], None,
                    outhbm, IndirectOffsetOnAxis(gi[:, tt:tt + 1], 0),
                    oob_is_err=False),
                    wait=w + [(ve, ve_gstt[gpar])], inc=[(sGB[gpar], 16)])
                sGB_cnt[gpar] = sGB[gpar].n
                r.emit("v", lambda v, tt=tt, p=gpar, wsl=wsl: v.scalar_tensor_tensor(
                    acc[:, tt, :], gstage[p][:], wsl[:, tt:tt + 1], acc[:, tt, :],
                    op0=ALU.mult, op1=ALU.add),
                    wait=[(sGB[gpar], sGB_cnt[gpar]), (ve, ve.n)], inc=[(ve, 1)])
                ve_gstt[gpar] = ve.n
        ve_exp = ve.n

        if stage <= 5:
            continue
        for tt in range(NW):
            r.emit("v", lambda v, tt=tt: v.reduce_sum(
                stats[:, 16:17], acc[:, tt, :], axis=AX.X, negate=True),
                wait=[(ve, ve_exp)], inc=[(ve, 1)])
            r.emit("v", lambda v, tt=tt: v.tensor_scalar_mul(
                st2b[:, tt:tt + 1], stats[:, 16:17], 1.0 / D),
                wait=[(ve, ve.n)], inc=[(ve, 1)])
            r.emit("v", lambda v, tt=tt: v.scalar_tensor_tensor(
                x1btm[:, 0, :], acc[:, tt, :], 1.0, acc[:, tt, :],
                op0=ALU.mult, op1=ALU.mult, accum_out=stats[:, 18:19]),
                wait=[(ve, ve.n)], inc=[(ve, 1)])
            r.emit("v", lambda v, tt=tt: v.tensor_mul(
                stats[:, 19:20], st2b[:, tt:tt + 1], st2b[:, tt:tt + 1]),
                wait=[(ve, ve.n)], inc=[(ve, 1)])
            r.emit("v", lambda v, tt=tt: v.tensor_scalar(
                ve8b[:, tt:tt + 1], stats[:, 18:19], 1.0 / D, stats[:, 19:20],
                op0=ALU.mult, op1=ALU.subtract),
                wait=[(ve, ve.n)], inc=[(ve, 1)])
        r.emit("s", lambda s: s.activation(sd8b[:], ve8b[:], AF.Sqrt, bias=cb[:, 0:1]),
               wait=[(ve, ve.n)], inc=[(se, 1)])
        r.emit("v", lambda v: v.reciprocal(rstd8b[:], sd8b[:]),
               wait=[(se, se.n)], inc=[(ve, 1)])
        for tt in range(NW):
            par = tt % 2
            r.emit("v", lambda v, tt=tt, p=par: v.tensor_scalar(
                outf[p][:], acc[:, tt, :], st2b[:, tt:tt + 1],
                rstd8b[:, tt:tt + 1], op0=ALU.add, op1=ALU.mult),
                wait=[(ve, ve.n), (sXO[par], sXO_cnt[par])], inc=[(ve, 1)])
            ve_outf[par] = ve.n
            row0 = c * CH + tt * 128
            r.emit("y", lambda y, r0=row0, p=par: y.dma_start(
                out_d[r0:r0 + 128, :], outf[p][:]),
                wait=[(ve, ve_outf[par])], inc=[(sXO[par], 16)])
            sXO_cnt[par] = sXO[par].n
        pe_endB = pe.n
        ve_endB = ve.n

    dma_sems = [sInit, swA, sMZ, sSC, sW13[0], sW13[1], sW13h[0], sW13h[1],
                sW2q[0], sW2q[1],
                sWC[0], sWC[1], sWC2[0], sWC2[1], sXT[0], sXT[1],
                sXF[0], sXF[1], sOB[0], sOB[1], sGB[0], sGB[1],
                sXO[0], sXO[1]]
    r.emit("y", lambda y: y.nop(),
           wait=[(s, s.n) for s in dma_sems] + [(pe, pe.n), (ve, ve.n), (se, se.n)])

    with nc.Block() as blk:
        def runner(key):
            def go(eng):
                for fn, waits, incs in r.ops[key]:
                    for sem, val in waits:
                        if val > 0:
                            eng.wait_ge(sem.h, val)
                    inst = fn(eng)
                    for sem, k in incs:
                        inst.then_inc(sem.h, k)
            return go
        blk.tensor(runner("t"))
        blk.vector(runner("v"))
        blk.scalar(runner("s"))
        blk.gpsimd(runner("g"))
        blk.sync(runner("y"))
    return nc


def make_consts():
    i = np.arange(W)[:, None]
    j = np.arange(2 * W)[None, :]
    mask = np.where(j - i > W, -3.4028235e38, 0.0).astype(np.float32)
    mask4 = np.ascontiguousarray(np.tile(mask, (1, 4)))
    ident = np.eye(128, dtype=np.float32)
    return mask4, ident


def prep_shared(wq, wkv, wo, bo, gw, w1, w3, w2):
    C = np.ascontiguousarray
    f32 = np.float32
    wq = np.asarray(wq, f32)
    wkv = np.asarray(wkv, f32)
    wo = np.asarray(wo, f32)
    gw = np.asarray(gw, f32)
    w1 = np.asarray(w1, f32)
    w3 = np.asarray(w3, f32)
    w2 = np.asarray(w2, f32)
    mask4, ident = make_consts()
    wqr = C(wq.reshape(DB, 128, D).transpose(1, 0, 2))
    wkvr = C(wkv.reshape(DB, 128, 2 * D).transpose(1, 0, 2))
    gwr = C(gw.reshape(DB, 128, E).transpose(1, 0, 2))
    wor = C(wo.reshape(4, 2, 2, 64, D).transpose(1, 3, 0, 2, 4).reshape(128, DB, D))
    def w13r(w):
        return C(w.reshape(E, DB, 128, 8, 512).transpose(0, 3, 2, 1, 4))
    w2r = C(w2.reshape(E, 4, 8, 128, D).transpose(0, 1, 3, 2, 4))
    bo_bc = C(np.broadcast_to(np.asarray(bo, f32), (128, D)))
    L128 = np.triu(np.ones((128, 128), f32), 1)
    onescol = np.ones((128, 1), f32)
    ones1 = np.ones((1, 128), f32)
    iota8 = C(np.broadcast_to(np.arange(E, dtype=f32), (128, E)))
    return {
        "wq": wqr, "wkv": wkvr, "wo": wor, "gw": gwr,
        "w1": w13r(w1), "w3": w13r(w3), "w2": w2r,
        "bo_bc": bo_bc, "mask4": mask4, "ident": ident,
        "L128": L128, "onescol": onescol, "ones1": ones1,
        "iota8": iota8,
    }


def kernel(L0_attn_x, L0_attn_wq, L0_attn_wkv, L0_attn_wo, L0_attn_bo,
           L0_ffn_gate_w, L0_ffn_w1, L0_ffn_w3, L0_ffn_w2):
    x = np.ascontiguousarray(np.asarray(L0_attn_x, dtype=np.float32))
    Bb, Nn, Dd = x.shape
    n_cores = 8
    TPC = Bb * Nn // n_cores
    n_chunks = TPC // CH
    shared = prep_shared(L0_attn_wq, L0_attn_wkv, L0_attn_wo, L0_attn_bo,
                         L0_ffn_gate_w, L0_ffn_w1, L0_ffn_w3, L0_ffn_w2)
    in_maps = []
    for core in range(n_cores):
        b, s0 = divmod(core * TPC, Nn)
        halo = (np.zeros((W, D), np.float32) if s0 == 0
                else x[b, s0 - W:s0])
        xp = np.ascontiguousarray(np.concatenate([halo, x[b, s0:s0 + TPC]], 0))
        m = dict(shared)
        m["xp"] = xp
        in_maps.append(m)

    nc = bass.Bass("TRN2", target_bir_lowering=False)
    build_core_program(nc, n_chunks)
    res = run_bass_kernel_spmd(nc, in_maps, core_ids=list(range(n_cores)))
    out = np.empty((Bb, Nn, D), np.float32)
    for core in range(n_cores):
        b, s0 = divmod(core * TPC, Nn)
        out[b, s0:s0 + TPC] = res.results[core]["out"]
    return out


# version 10
# speedup vs baseline: 1.0506x; 1.0506x over previous
import numpy as np
import concourse.bass as bass
import concourse.mybir as mybir
from concourse.bass_utils import run_bass_kernel_spmd
from concourse.bass_core_rust import IndirectOffsetOnAxis

F32 = mybir.dt.float32
BF16 = mybir.dt.bfloat16
AF = mybir.ActivationFunctionType
ALU = mybir.AluOpType
AX = mybir.AxisListType

D = 1024
H, DH, W = 16, 64, 128
E, F = 8, 4096
CH = 1024
NW = CH // W
DB = D // 128
FB = 16
SCALE = DH ** -0.5
EPS = 1e-5
USE_WCACHE = False
C_CAP = 448
NSL = 448
SLOT = 512
XE_ROWS = E * SLOT + SLOT

SBUF_BASE = 16512
SBUF_TOP = 229344


class Sem:
    def __init__(self, nc, name):
        self.h = nc.alloc_semaphore(name)
        self.n = 0


class Rec:
    ENGS = ("t", "v", "s", "g", "y")

    def __init__(self):
        self.ops = {e: [] for e in self.ENGS}

    def emit(self, eng, fn, wait=(), inc=()):
        for sem, val in wait:
            assert val <= sem.n, f"wait on {sem.h.name} for {val} > recorded {sem.n}"
        self.ops[eng].append((fn, tuple(wait), tuple(inc)))
        for sem, k in inc:
            sem.n += k


def build_core_program(nc: bass.Bass, n_chunks: int, stage: int = 99, wstage: int = 99):
    TPC = n_chunks * CH

    xp = nc.dram_tensor("xp", [W + TPC, D], F32, kind="ExternalInput").ap()
    wq_d = nc.dram_tensor("wq", [128, DB, D], F32, kind="ExternalInput").ap()
    wkv_d = nc.dram_tensor("wkv", [128, DB, 2 * D], F32, kind="ExternalInput").ap()
    wo_d = nc.dram_tensor("wo", [128, DB, D], F32, kind="ExternalInput").ap()
    gw_d = nc.dram_tensor("gw", [128, DB, E], F32, kind="ExternalInput").ap()
    w1_d = nc.dram_tensor("w1", [E, 8, 128, DB, 512], F32, kind="ExternalInput").ap()
    w3_d = nc.dram_tensor("w3", [E, 8, 128, DB, 512], F32, kind="ExternalInput").ap()
    w2_d = nc.dram_tensor("w2", [E, 4, 128, 8, D], F32, kind="ExternalInput").ap()
    L128_d = nc.dram_tensor("L128", [128, 128], F32, kind="ExternalInput").ap()
    onescol_d = nc.dram_tensor("onescol", [128, 1], F32, kind="ExternalInput").ap()
    ones1_d = nc.dram_tensor("ones1", [1, 128], F32, kind="ExternalInput").ap()
    iota8_d = nc.dram_tensor("iota8", [128, E], F32, kind="ExternalInput").ap()
    xehbm = nc.dram_tensor("xehbm", [XE_ROWS, D], BF16, kind="Internal").ap()
    w1c = nc.dram_tensor("w1c", [E, 8, 128, DB, 512], BF16, kind="Internal").ap()
    w3c = nc.dram_tensor("w3c", [E, 8, 128, DB, 512], BF16, kind="Internal").ap()
    w2c = nc.dram_tensor("w2c", [E, 4, 128, 8, D], BF16, kind="Internal").ap()
    outhbm = nc.dram_tensor("outhbm", [XE_ROWS, D], F32, kind="Internal").ap()
    bo_bc_d = nc.dram_tensor("bo_bc", [128, D], F32, kind="ExternalInput").ap()
    mask4_d = nc.dram_tensor("mask4", [128, 4 * 256], F32, kind="ExternalInput").ap()
    ident_d = nc.dram_tensor("ident", [128, 128], F32, kind="ExternalInput").ap()
    out_d = nc.dram_tensor("out", [TPC, D], F32, kind="ExternalOutput").ap()

    off = [SBUF_BASE]

    def sb(name, shape, dtype, at=None):
        nbytes = int(np.prod(shape[1:])) * mybir.dt.size(dtype)
        if at is None:
            at = off[0]
            off[0] += (nbytes + 63) & ~63
        t = nc.alloc_sbuf_tensor_at(name, shape, dtype, offset=at)
        assert at + nbytes <= SBUF_TOP, f"{name} overflows sbuf: {at}+{nbytes}"
        return t

    acc = sb("acc", [128, NW, D], F32)
    bo_b = sb("bo_b", [128, D], F32)
    mask4 = sb("mask4s", [128, 4 * 256], F32)
    ident = sb("idents", [128, 128], BF16)
    gwb = sb("gwb", [128, DB, E], BF16)
    wi = sb("wi", [128, NW, E], F32)
    stats = sb("stats", [128, 64], F32)
    st2 = sb("st2", [128, NW], F32)
    ve8 = sb("ve8", [128, NW], F32)
    sd8 = sb("sd8", [128, NW], F32)
    rstd8 = sb("rstd8", [128, NW], F32)
    st2b = sb("st2b", [128, NW], F32)
    ve8b = sb("ve8b", [128, NW], F32)
    sd8b = sb("sd8b", [128, NW], F32)
    rstd8b = sb("rstd8b", [128, NW], F32)
    lgblk = sb("lgblk", [128, 384], F32)
    posf = sb("posf", [128, 64], F32)
    L128s = sb("L128s", [128, 128], F32)
    onescol = sb("onescol", [128, 1], F32)
    ones1s = sb("ones1s", [1, 128], F32)
    iota8s = sb("iota8s", [128, E], F32)
    I32 = mybir.dt.int32
    s1i = sb("s1i", [128, NW], I32)
    s2i = sb("s2i", [128, NW], I32)
    g1i = sb("g1i", [128, NW], I32)
    g2i = sb("g2i", [128, NW], I32)
    w1sel = sb("w1sel", [128, NW], F32)
    w2sel = sb("w2sel", [128, NW], F32)
    ttlrow = sb("ttlrow", [1, NW, E], F32)
    baserow = sb("baserow", [1, NW, E], F32)
    cb = sb("cb", [128, 2], F32)
    shared0 = off[0]

    offA = [shared0]
    offsave = off[0]

    def sba(name, shape, dtype):
        off[0] = offA[0]
        t = sb(name, shape, dtype)
        offA[0] = off[0]
        return t

    xT = sba("xT", [128, DB, W + CH], BF16)
    qT = sba("qT", [128, DB, CH], BF16)
    kT = sba("kT", [128, DB, W + CH], BF16)
    vb = sba("vb", [128, NW + 1, D], BF16)
    wqb = sba("wqb", [128, DB, D], BF16)
    wkvb = sba("wkvb", [128, DB, 2 * D], BF16)
    wob = sba("wob", [128, DB, D], BF16)
    xb = [sba(f"xb{i}", [128, D], BF16) for i in range(2)]
    e4a = sba("e4a", [128, 4 * 256], F32)
    e4b = sba("e4b", [128, 4 * 256], F32)
    p4 = sba("p4", [128, 4, 256], BF16)
    pTs = sba("pTs", [128, 4, 2, 128], BF16)
    oT = sba("oT", [128, DB, 128], BF16)
    ob = sba("ob", [128, 256], BF16)
    xf = [sba(f"xf{i}", [128, D], F32) for i in range(2)]
    xg = xf
    a_end = offA[0]

    offB = [shared0]

    def sbb_(name, shape, dtype):
        off[0] = offB[0]
        t = sb(name, shape, dtype)
        offB[0] = off[0]
        return t

    x1T = sbb_("x1T", [128, DB, CH], BF16)
    x1btm = sbb_("x1btm", [128, NW, D], BF16)
    xeT = [sbb_(f"xeT{i}", [128, DB, SLOT], BF16) for i in range(2)]
    hbuf = sbb_("hbuf", [128, 32, NSL], BF16)
    w2b = [sbb_(f"w2b{i}", [128, 8, D], BF16) for i in range(2)]
    w1b = [sbb_(f"w1b{i}", [128, DB, 512], BF16) for i in range(2)]
    w3b = [sbb_(f"w3b{i}", [128, DB, 512], BF16) for i in range(2)]
    sg = [sbb_(f"sg{i}", [128, 512], F32) for i in range(2)]
    ostg = [sbb_(f"ostg{i}", [128, D], F32) for i in range(2)]
    gstage = ostg
    outf = ostg
    b_end = offB[0]
    off[0] = offsave
    assert max(a_end, b_end) <= SBUF_TOP, (a_end, b_end)

    PA = nc.alloc_psum_tensor("PA", [128, 1024], F32)
    PB = nc.alloc_psum_tensor("PB", [128, 1024], F32)
    PC = nc.alloc_psum_tensor("PC", [128, 512], F32)
    PD = nc.alloc_psum_tensor("PD", [128, 512], F32)
    PE2 = nc.alloc_psum_tensor("PE2", [128, 512], F32)
    PF = nc.alloc_psum_tensor("PF", [128, 512], F32)
    PCb = PC[:].bitcast(BF16)
    PDb = PD[:].bitcast(BF16)
    g1p = [PC, PE2]
    g3p = [PD, PF]
    outp = [PA, PB]

    pe = Sem(nc, "pe")
    ve = Sem(nc, "ve")
    se = Sem(nc, "se")
    sInit = Sem(nc, "sInit")
    swA = Sem(nc, "swA")
    sW13 = [Sem(nc, "sW13a"), Sem(nc, "sW13b")]
    sW2q = [Sem(nc, "sW2a"), Sem(nc, "sW2b")]
    sW13h = [Sem(nc, "sW13ha"), Sem(nc, "sW13hb")]
    sWC = [Sem(nc, "sWCa"), Sem(nc, "sWCb")]
    sWC2 = [Sem(nc, "sWC2a"), Sem(nc, "sWC2b")]
    sSC = Sem(nc, "sSC")
    sXT = [Sem(nc, "sXTa"), Sem(nc, "sXTb")]
    sOB = [Sem(nc, "sOBa"), Sem(nc, "sOBb")]
    sGB = [Sem(nc, "sGBa"), Sem(nc, "sGBb")]
    sMZ = Sem(nc, "sMZ")
    sXF = [Sem(nc, "sXFa"), Sem(nc, "sXFb")]
    sXG = sXF
    sXO = [Sem(nc, "sXOa"), Sem(nc, "sXOb")]

    r = Rec()

    r.emit("g", lambda g: g.dma_start(ident[:], ident_d), inc=[(sInit, 16)])
    r.emit("g", lambda g: g.dma_start(mask4[:], mask4_d), inc=[(sInit, 16)])
    r.emit("g", lambda g: g.dma_start(bo_b[:], bo_bc_d), inc=[(sInit, 16)])
    r.emit("g", lambda g: g.dma_start(gwb[:], gw_d), inc=[(sInit, 16)])
    r.emit("g", lambda g: g.dma_start(L128s[:], L128_d), inc=[(sInit, 16)])
    r.emit("g", lambda g: g.dma_start(onescol[:], onescol_d), inc=[(sInit, 16)])
    r.emit("g", lambda g: g.dma_start(ones1s[:], ones1_d), inc=[(sInit, 16)])
    r.emit("g", lambda g: g.dma_start(iota8s[:], iota8_d), inc=[(sInit, 16)])
    sInitN = sInit.n
    r.emit("v", lambda v: v.memset(cb[:, 0:1], EPS), inc=[(ve, 1)])
    r.emit("v", lambda v: v.memset(cb[:, 1:2], SCALE), inc=[(ve, 1)])
    ve_cb = ve.n
    r.emit("v", lambda v: v.memset(x1btm[:, 0, :], 0.0), inc=[(ve, 1)])
    r.emit("v", lambda v: v.memset(ostg[0][:], 0.0), inc=[(ve, 1)])
    ve_mz = ve.n
    xer = xehbm.rearrange("(n p) d -> n p d", p=128)
    for n in range(XE_ROWS // 128):
        r.emit("g", lambda g, n=n: g.dma_start(xer[n], x1btm[:, 0, :]),
               wait=[(ve, ve_mz)], inc=[(sMZ, 16)])
    ohr = outhbm.rearrange("(n p) d -> n p d", p=128)
    for n in range(XE_ROWS // 128):
        r.emit("g", lambda g, n=n: g.dma_start(ohr[n], ostg[0][:]),
               wait=[(ve, ve_mz)], inc=[(sMZ, 16)])
    sMZ_full = sMZ.n

    ve_xbcopy = [0, 0]
    pe_xtr = [0, 0]
    ve_trevac = [0, 0]
    se_qkvevac = [0, 0]
    ve_maskadd = 0
    ve_pTevac = 0
    ve_oevac = 0
    ve_oTevac = 0
    ve_epiTT1 = 0
    ve_TT2 = [0, 0]
    sXG_cnt = [0, 0]
    sXF_cnt = [0, 0]
    sXO_cnt = [0, 0]
    ve_outf = [0, 0]
    pe_endB = 0
    ve_endB = 0
    pe_w13use = [0, 0]
    pe_w2q = [0, 0]
    ve_hmul = [0, 0]
    se_silu = [0, 0]
    ve_lgcopy = 0
    ve_ttl = 0
    ve_posf = 0
    ve_outev = [0, 0, 0, 0]
    sOB_cnt = [0, 0]
    sGB_cnt = [0, 0]
    ve_gstt = [0, 0]
    pe_hlast = [0, 0]
    sXT_cnt = [0, 0]
    ve_x1bT = 0
    pe_x1tr = 0
    pe_lastout = 0

    for c in range(n_chunks):
        gate = [(pe, pe_endB), (ve, ve_endB),
                (sXO[0], sXO_cnt[0]), (sXO[1], sXO_cnt[1])]
        r.emit("g", lambda g: g.dma_start(wqb[:], wq_d),
               wait=gate, inc=[(swA, 16)])
        r.emit("g", lambda g: g.dma_start(wkvb[:], wkv_d), inc=[(swA, 16)])
        r.emit("g", lambda g: g.dma_start(wob[:], wo_d), inc=[(swA, 16)])
        swA_full = swA.n

        for i in range(NW + 1):
            par = i % 2
            row0 = c * CH + i * 128
            r.emit("y", lambda y, r0=row0, p=par: y.dma_start(
                xf[p][:], xp[r0:r0 + 128, :]),
                wait=[(ve, ve_xbcopy[par]), (ve, ve_TT2[par]), (pe, pe_endB)],
                inc=[(sXF[par], 16)])
            sXF_cnt[par] = sXF[par].n
            r.emit("v", lambda v, p=par: v.tensor_copy(xb[p][:], xf[p][:]),
                   wait=[(sXF[par], sXF_cnt[par]), (pe, pe_xtr[par])], inc=[(ve, 1)])
            ve_xbcopy[par] = ve.n
            scr = PCb if par == 0 else PDb

            def xtr(t, p=par, s=scr):
                for db in range(DB):
                    inst = t.transpose(s[:, db * 128:(db + 1) * 128],
                                       xb[p][:, db * 128:(db + 1) * 128], ident[:])
                return inst
            r.emit("t", xtr, wait=[(ve, ve_xbcopy[par]), (ve, ve_trevac[par]),
                                   (sInit, sInitN)], inc=[(pe, 1)])
            pe_xtr[par] = pe.n
            r.emit("v", lambda v, i=i, s=scr: v.tensor_copy(
                xT[:, :, i * 128:(i + 1) * 128],
                s.rearrange("p (db t) -> p db t", db=DB)),
                wait=[(pe, pe.n), (sMZ, sMZ_full)], inc=[(ve, 1)])
            ve_trevac[par] = ve.n
        ve_xT = ve.n

        if stage <= 1:
            continue
        def proj(dst_evac, lhs_w, lhs_col0, rhs_src, rhs_off, n_cols, psum_par,
                 scale=None, tok_lhs=False):
            P = [PC, PD][psum_par]

            def mm(t):
                for db in range(DB):
                    if tok_lhs:
                        lhsT = xT[:, db, lhs_col0:lhs_col0 + 128]
                        rhs = rhs_src[:, db, rhs_off:rhs_off + n_cols]
                    else:
                        lhsT = lhs_w[:, db, lhs_col0:lhs_col0 + 128]
                        rhs = rhs_src[:, db, rhs_off:rhs_off + n_cols]
                    inst = t.matmul(P[:, 0:n_cols], lhsT, rhs,
                                    start=(db == 0), stop=(db == DB - 1))
                return inst
            r.emit("t", mm, wait=[(ve, ve_xT), (swA, swA_full),
                                  (se, se_qkvevac[psum_par])], inc=[(pe, 1)])

            def evac(s, P=P, n=n_cols):
                if scale is None:
                    return s.copy(dst_evac, P[:, 0:n])
                return s.activation(dst_evac, P[:, 0:n], AF.Copy, scale=cb[:, 1:2])
            r.emit("s", evac, wait=[(pe, pe.n), (ve, ve_cb)], inc=[(se, 1)])
            se_qkvevac[psum_par] = se.n

        pp = 0
        for ib in range(DB):
            for th in range(2):
                proj(qT[:, ib, th * 512:(th + 1) * 512], wqb, ib * 128,
                     xT, W + th * 512, 512, pp % 2, scale=SCALE)
                pp += 1
        for ib in range(DB):
            for (o0, nc_) in ((0, 512), (512, 512), (1024, 128)):
                proj(kT[:, ib, o0:o0 + nc_], wkvb, ib * 128,
                     xT, o0, nc_, pp % 2)
                pp += 1
        for tb in range(NW + 1):
            for nh in range(2):
                proj(vb[:, tb, nh * 512:(nh + 1) * 512], None, tb * 128,
                     wkvb, D + nh * 512, 512, pp % 2, tok_lhs=True)
                pp += 1
        se_qkv = se.n

        if stage <= 2:
            continue
        for w in range(NW):
            for hb in range(4):
                def sims(t, w=w, hb=hb):
                    for s in range(4):
                        h = hb * 4 + [0, 2, 1, 3][s]
                        hp, hw = h % 2, h // 2
                        lhsT = qT[hp * 64:(hp + 1) * 64, hw, w * 128:(w + 1) * 128]
                        rhs = kT[hp * 64:(hp + 1) * 64, hw, w * 128:w * 128 + 256]
                        inst = t.matmul(PA[:, s * 256:(s + 1) * 256], lhsT, rhs,
                                        start=True, stop=True)
                    return inst
                r.emit("t", sims, wait=[(se, se_qkv), (ve, ve_maskadd)],
                       inc=[(pe, 1)])
                if wstage <= 0:
                    continue
                r.emit("v", lambda v: v.tensor_add(e4b[:], PA[:], mask4[:]),
                       wait=[(pe, pe.n), (sInit, sInitN), (sMZ, sMZ_full)],
                       inc=[(ve, 1)])
                ve_maskadd = ve.n
                if wstage <= -1:
                    pass
                r.emit("v", lambda v: v.reduce_max(
                    stats[:, 0:4], e4b[:].rearrange("p (s x) -> p s x", s=4),
                    axis=AX.X, negate=True), wait=[(ve, ve.n)], inc=[(ve, 1)])

                def exps(s):
                    for hh in range(4):
                        inst = s.activation(e4a[:, hh * 256:(hh + 1) * 256],
                                            e4b[:, hh * 256:(hh + 1) * 256],
                                            AF.Exp, bias=stats[:, hh:hh + 1],
                                            accum_out=stats[:, 8 + hh:9 + hh])
                    return inst
                if wstage <= 1:
                    continue
                r.emit("s", exps, wait=[(ve, ve.n)], inc=[(se, 1)])
                r.emit("v", lambda v: v.reciprocal(stats[:, 12:16], stats[:, 8:12]),
                       wait=[(se, se.n)], inc=[(ve, 1)])

                def pnorm(v):
                    for hh in range(4):
                        inst = v.tensor_scalar_mul(
                            p4[:, hh, :], e4a[:, hh * 256:(hh + 1) * 256],
                            stats[:, 12 + hh:13 + hh])
                    return inst
                r.emit("v", pnorm, wait=[(ve, ve.n)], inc=[(ve, 1)])
                ve_p4 = ve.n

                def ptr(t):
                    for hh in range(4):
                        for jh in range(2):
                            inst = t.transpose(
                                PCb[:, (hh * 2 + jh) * 128:(hh * 2 + jh + 1) * 128],
                                p4[:, hh, jh * 128:(jh + 1) * 128], ident[:])
                    return inst
                if wstage <= 2:
                    continue
                r.emit("t", ptr, wait=[(ve, ve_p4), (ve, ve_pTevac)], inc=[(pe, 1)])
                r.emit("v", lambda v: v.tensor_copy(
                    pTs[:].rearrange("p a b t -> p (a b t)"), PCb[:]),
                    wait=[(pe, pe.n)], inc=[(ve, 1)])
                ve_pTevac = ve.n

                if wstage <= 3:
                    continue

                def omm(t, w=w, hb=hb):
                    for s in range(4):
                        h = hb * 4 + [0, 2, 1, 3][s]
                        for jh in range(2):
                            inst = t.matmul(
                                PE2[:, s * 64:(s + 1) * 64],
                                pTs[:, s, jh, :],
                                vb[:, w + jh, h * 64:(h + 1) * 64],
                                start=(jh == 0), stop=(jh == 1))
                    return inst
                r.emit("t", omm, wait=[(ve, ve_pTevac), (ve, ve_oevac)],
                       inc=[(pe, 1)])
                r.emit("v", lambda v: v.tensor_copy(ob[:], PE2[:, 0:256]),
                       wait=[(pe, pe.n)], inc=[(ve, 1)])
                ve_oevac = ve.n

                def otr(t, hb=hb):
                    for k in range(2):
                        inst = t.transpose(
                            PDb[:, (hb * 2 + k) * 128:(hb * 2 + k + 1) * 128],
                            ob[:, k * 128:(k + 1) * 128], ident[:])
                    return inst
                r.emit("t", otr, wait=[(ve, ve_oevac), (ve, ve_oTevac)],
                       inc=[(pe, 1)])
            if wstage <= 4:
                continue
            r.emit("v", lambda v: v.tensor_copy(
                oT[:].rearrange("p a t -> p (a t)"), PDb[:]),
                wait=[(pe, pe.n)], inc=[(ve, 1)])
            ve_oTevac = ve.n

            def womm(t):
                for nh in range(2):
                    for ib in range(DB):
                        inst = t.matmul(PB[:, nh * 512:(nh + 1) * 512],
                                        oT[:, ib, :],
                                        wob[:, ib, nh * 512:(nh + 1) * 512],
                                        start=(ib == 0), stop=(ib == DB - 1))
                return inst
            r.emit("t", womm, wait=[(ve, ve_oTevac), (swA, swA_full),
                                    (ve, ve_epiTT1)], inc=[(pe, 1)])
            if wstage <= 5:
                continue
            par = w % 2
            row0 = W + c * CH + w * 128
            r.emit("y", lambda y, r0=row0, p=par: y.dma_start(
                xg[p][:], xp[r0:r0 + 128, :]),
                wait=[(ve, ve_TT2[par]), (ve, ve_xT)], inc=[(sXG[par], 16)])
            sXG_cnt[par] = sXG[par].n
            r.emit("v", lambda v, w=w: v.tensor_add(acc[:, w, :], PB[:], bo_b[:]),
                   wait=[(pe, pe.n), (sInit, sInitN)], inc=[(ve, 1)])
            ve_epiTT1 = ve.n
            r.emit("v", lambda v, w=w, p=par: v.tensor_add(
                acc[:, w, :], acc[:, w, :], xg[p][:]),
                wait=[(sXG[par], sXG_cnt[par]), (ve, ve.n)], inc=[(ve, 1)])
            ve_TT2[par] = ve.n
            r.emit("v", lambda v, w=w: v.reduce_sum(
                stats[:, 16:17], acc[:, w, :], axis=AX.X, negate=True),
                wait=[(ve, ve.n)], inc=[(ve, 1)])
            r.emit("v", lambda v, w=w: v.tensor_scalar_mul(
                st2[:, w:w + 1], stats[:, 16:17], 1.0 / D),
                wait=[(ve, ve.n)], inc=[(ve, 1)])
            r.emit("v", lambda v, w=w: v.scalar_tensor_tensor(
                e4a[:], acc[:, w, :], 1.0, acc[:, w, :],
                op0=ALU.mult, op1=ALU.mult, accum_out=stats[:, 18:19]),
                wait=[(ve, ve.n)], inc=[(ve, 1)])
            r.emit("v", lambda v, w=w: v.tensor_mul(
                stats[:, 19:20], st2[:, w:w + 1], st2[:, w:w + 1]),
                wait=[(ve, ve.n)], inc=[(ve, 1)])
            r.emit("v", lambda v, w=w: v.tensor_scalar(
                ve8[:, w:w + 1], stats[:, 18:19], 1.0 / D, stats[:, 19:20],
                op0=ALU.mult, op1=ALU.subtract),
                wait=[(ve, ve.n)], inc=[(ve, 1)])
        if wstage <= 5:
            continue
        r.emit("s", lambda s: s.activation(sd8[:], ve8[:], AF.Sqrt, bias=cb[:, 0:1]),
               wait=[(ve, ve.n)], inc=[(se, 1)])
        r.emit("v", lambda v: v.reciprocal(rstd8[:], sd8[:]),
               wait=[(se, se.n)], inc=[(ve, 1)])
        for w in range(NW):
            r.emit("v", lambda v, w=w: v.tensor_scalar(
                acc[:, w, :], acc[:, w, :], st2[:, w:w + 1], rstd8[:, w:w + 1],
                op0=ALU.add, op1=ALU.mult),
                wait=[(ve, ve.n)], inc=[(ve, 1)])
        ve_x1 = ve.n

        if stage <= 3:
            continue
        pe_womm = pe.n
        hoist13 = {}
        hoistw2 = {}
        ve_hoist = ve.n
        w1s, w3s, w2s = ((w1_d, w3_d, w2_d) if (c == 0 or not USE_WCACHE)
                         else (w1c, w3c, w2c))

        def wb13(e, f8):
            if c != 0 or not USE_WCACHE:
                return
            par = f8 % 2
            r.emit("y", lambda y, e=e, f8=f8, par=par: y.dma_start(
                w1c[e, f8], w1b[par][:]),
                wait=[(sW13[par], sW13[par].n)], inc=[(sWC[par], 16)])
            r.emit("y", lambda y, e=e, f8=f8, par=par: y.dma_start(
                w3c[e, f8], w3b[par][:]), inc=[(sWC[par], 16)])

        def wb2(e, q):
            if c != 0 or not USE_WCACHE:
                return
            par = q % 2
            r.emit("y", lambda y, e=e, q=q, par=par: y.dma_start(
                w2c[e, q], w2b[par][:]),
                wait=[(sW2q[par], sW2q[par].n)], inc=[(sWC2[par], 16)])

        for f8 in range(2):
            r.emit("g", lambda g, f8=f8, w1s=w1s: g.dma_start(
                w1b[f8 % 2][:], w1s[0, f8]),
                wait=[(pe, pe_womm), (pe, pe_w13use[f8 % 2]), (ve, ve_hoist),
                      (sWC[f8 % 2], sWC[f8 % 2].n)],
                inc=[(sW13[f8 % 2], 16)])
            r.emit("g", lambda g, f8=f8, w3s=w3s: g.dma_start(
                w3b[f8 % 2][:], w3s[0, f8]),
                inc=[(sW13[f8 % 2], 16)])
            hoist13[f8] = sW13[f8 % 2].n
            wb13(0, f8)
        for q in range(2):
            r.emit("g", lambda g, q=q, w2s=w2s: g.dma_start(
                w2b[q % 2][:], w2s[0, q]),
                wait=[(pe, pe_womm), (pe, pe_w2q[q % 2]), (ve, ve_hoist),
                      (sWC2[q % 2], sWC2[q % 2].n)],
                inc=[(sW2q[q % 2], 16)])
            hoistw2[q] = sW2q[q % 2].n
            wb2(0, q)
        for tt in range(NW):
            par = tt % 2
            scr = PCb if par == 0 else PDb
            r.emit("v", lambda v, tt=tt: v.tensor_copy(x1btm[:, tt, :], acc[:, tt, :]),
                   wait=[(ve, ve_x1), (sSC, sSC.n)], inc=[(ve, 1)])
            ve_x1bT = ve.n

            def x1tr(t, s=scr, tt=tt):
                for db in range(DB):
                    inst = t.transpose(s[:, db * 128:(db + 1) * 128],
                                       x1btm[:, tt, db * 128:(db + 1) * 128], ident[:])
                return inst
            r.emit("t", x1tr, wait=[(ve, ve_x1bT), (ve, ve_trevac[par])],
                   inc=[(pe, 1)])
            pe_x1tr = pe.n
            r.emit("v", lambda v, tt=tt, s=scr: v.tensor_copy(
                x1T[:, :, tt * 128:(tt + 1) * 128],
                s.rearrange("p (db t) -> p db t", db=DB)),
                wait=[(pe, pe.n)], inc=[(ve, 1)])
            ve_trevac[par] = ve.n
        ve_x1T = ve.n

        import dataclasses as _dc

        def bc8(ap2d):
            return _dc.replace(ap2d, ap=[list(ap2d.ap[0]), list(ap2d.ap[1]), [0, 8]])

        def seg(ap2d):
            return ap2d.rearrange("p (s x) -> p s x", s=8)

        for tt in range(NW):
            def gmm(t, tt=tt):
                for db in range(DB):
                    inst = t.matmul(PE2[:, tt * 8:(tt + 1) * 8],
                                    x1T[:, db, tt * 128:(tt + 1) * 128],
                                    gwb[:, db, :], start=(db == 0), stop=(db == DB - 1))
                return inst
            r.emit("t", gmm, wait=[(ve, ve_x1T), (sInit, sInitN), (ve, ve_lgcopy)],
                   inc=[(pe, 1)])
        r.emit("v", lambda v: v.tensor_copy(lgblk[:, 0:64], PE2[:, 0:64]),
               wait=[(pe, pe.n)], inc=[(ve, 1)])
        ve_lgcopy = ve.n
        L = lgblk

        def chain(fn, eng="v"):
            r.emit(eng, fn, wait=[(ve, ve.n)] if eng == "v" else [(ve, ve.n)],
                   inc=[(ve, 1) if eng == "v" else (se, 1)])
        r.emit("s", lambda s: s.activation(L[:, 64:128], L[:, 0:64], AF.Exp),
               wait=[(ve, ve.n)], inc=[(se, 1)])
        r.emit("v", lambda v: v.reduce_max(L[:, 256:264], seg(L[:, 64:128]),
                                           axis=AX.X, negate=True),
               wait=[(se, se.n)], inc=[(ve, 1)])
        chain(lambda v: v.tensor_scalar_mul(L[:, 264:272], L[:, 256:264], -1.0))
        chain(lambda v: v.tensor_tensor(seg(L[:, 128:192]), seg(L[:, 64:128]),
                                        bc8(L[:, 264:272]), op=ALU.is_equal))
        chain(lambda v: v.scalar_tensor_tensor(L[:, 192:256], L[:, 128:192], -1e30,
                                               L[:, 64:128], op0=ALU.mult, op1=ALU.add))
        chain(lambda v: v.reduce_max(L[:, 272:280], seg(L[:, 192:256]),
                                     axis=AX.X, negate=True))
        chain(lambda v: v.tensor_scalar_mul(L[:, 280:288], L[:, 272:280], -1.0))
        chain(lambda v: v.tensor_tensor(seg(L[:, 192:256]), seg(L[:, 64:128]),
                                        bc8(L[:, 280:288]), op=ALU.is_ge))
        chain(lambda v: v.tensor_add(L[:, 288:296], L[:, 264:272], L[:, 280:288]))
        chain(lambda v: v.reciprocal(L[:, 296:304], L[:, 288:296]))
        chain(lambda v: v.tensor_mul(w1sel[:].rearrange("p a -> p a"), L[:, 264:272],
                                     L[:, 296:304]))
        chain(lambda v: v.tensor_mul(w2sel[:].rearrange("p a -> p a"), L[:, 280:288],
                                     L[:, 296:304]))
        ve_sel = ve.n
        for tt in range(NW):
            r.emit("t", lambda t, tt=tt: t.matmul(
                PF[0:1, tt * 8:(tt + 1) * 8], onescol[:],
                L[:, 192 + tt * 8:192 + (tt + 1) * 8], start=True, stop=True),
                wait=[(ve, ve_sel), (ve, ve_ttl)], inc=[(pe, 1)])
        r.emit("v", lambda v: v.tensor_copy(ttlrow[0:1, :, :].rearrange(
            "p a b -> p (a b)"), PF[0:1, 0:64]), wait=[(pe, pe.n)], inc=[(ve, 1)])
        ve_ttl = ve.n
        r.emit("v", lambda v: v.memset(baserow[0:1, 0, :], 0.0),
               wait=[(ve, ve.n)], inc=[(ve, 1)])
        for tt in range(1, NW):
            chain(lambda v, tt=tt: v.tensor_add(
                baserow[0:1, tt, :], baserow[0:1, tt - 1, :],
                ttlrow[0:1, tt - 1, :]))
        ve_base = ve.n
        for tt in range(NW):
            def posmm(t, tt=tt):
                t.matmul(PC[:, tt * 8:(tt + 1) * 8], L128s[:],
                         L[:, 192 + tt * 8:192 + (tt + 1) * 8],
                         start=True, stop=False)
                return t.matmul(PC[:, tt * 8:(tt + 1) * 8], ones1s[:],
                                baserow[0:1, tt, :], start=False, stop=True)
            r.emit("t", posmm, wait=[(ve, ve_base), (ve, ve_posf)], inc=[(pe, 1)])
        r.emit("v", lambda v: v.tensor_copy(posf[:], PC[:, 0:64]),
               wait=[(pe, pe.n)], inc=[(ve, 1)])
        ve_posf = ve.n
        for mcol, gi, si, wsl in ((264, g1i, s1i, w1sel), (280, g2i, s2i, w2sel)):
            chain(lambda v, m=mcol: v.tensor_tensor(
                seg(L[:, 128:192]), seg(L[:, 64:128]), bc8(L[:, m:m + 8]),
                op=ALU.is_equal))
            chain(lambda v: v.tensor_tensor(
                seg(L[:, 320:384]), seg(L[:, 128:192]),
                _dc.replace(iota8s[:], ap=[list(iota8s[:].ap[0]), [0, 8], [1, 8]]),
                op=ALU.mult))
            chain(lambda v: v.reduce_sum(L[:, 304:312], seg(L[:, 320:384]),
                                         axis=AX.X))
            chain(lambda v: v.tensor_mul(L[:, 320:384], L[:, 128:192], posf[:]))
            chain(lambda v: v.reduce_sum(L[:, 312:320], seg(L[:, 320:384]),
                                         axis=AX.X))
            chain(lambda v: v.tensor_scalar_mul(L[:, 304:312], L[:, 304:312],
                                                float(SLOT)))
            chain(lambda v: v.tensor_add(L[:, 304:312], L[:, 304:312],
                                         L[:, 312:320]))
            chain(lambda v, gi=gi: v.tensor_copy(gi[:], L[:, 304:312]))
            chain(lambda v: v.tensor_single_scalar(L[:, 328:336], L[:, 312:320],
                                                   float(C_CAP), ALU.is_lt))
            chain(lambda v, wsl=wsl: v.tensor_mul(wsl[:], wsl[:], L[:, 328:336]))
            chain(lambda v: v.tensor_scalar(L[:, 336:344], L[:, 328:336], -1.0, 1.0,
                                            op0=ALU.mult, op1=ALU.add))
            chain(lambda v: v.tensor_mul(L[:, 344:352], L[:, 304:312],
                                         L[:, 328:336]))
            chain(lambda v: v.scalar_tensor_tensor(
                L[:, 304:312], L[:, 336:344], float(XE_ROWS - 1), L[:, 344:352],
                op0=ALU.mult, op1=ALU.add))
            chain(lambda v, si=si: v.tensor_copy(si[:], L[:, 304:312]))
        ve_sidx = ve.n
        for tt in range(NW):
            for si in (s1i, s2i):
                wsc = [(ve, ve_sidx)]
                if tt == 0 and si is s1i:
                    wsc += [(sMZ, sMZ_full), (sXT[0], sXT_cnt[0]),
                            (sXT[1], sXT_cnt[1])]
                r.emit("g", lambda g, si=si, tt=tt: g.indirect_dma_start(
                    xehbm, IndirectOffsetOnAxis(si[:, tt:tt + 1], 0),
                    x1btm[:, tt, :], None, oob_is_err=False),
                    wait=wsc, inc=[(sSC, 16)])
        ve_gate = ve.n

        if stage <= 4:
            continue
        sSC_all = sSC.n
        for e in range(E):
            epar = e % 2
            for db in range(DB):
                r.emit("s", lambda s, e=e, db=db, p=epar: s.dma_start(
                    xeT[p][:, db, :],
                    xehbm[e * SLOT:(e + 1) * SLOT, db * 128:(db + 1) * 128],
                    transpose=True),
                    wait=[(sSC, sSC_all), (pe, pe_hlast[epar])] if db == 0 else [],
                    inc=[(sXT[epar], 16)])
            sXT_cnt[epar] = sXT[epar].n
            for f8 in range(8):
                par = f8 % 2

                def wload(wd, dst, p=par, e=e, f8=f8):
                    def go(g, wd=wd, dst=dst, p=p, e=e, f8=f8):
                        return g.dma_start(dst[p][:], wd[e, f8])
                    return go
                if e == 0 and f8 < 2:
                    s13_full = hoist13[f8]
                    s13_half = (sW13[par], s13_full)
                else:
                    def wloadh(wd, dst, half, p=par, e=e, f8=f8):
                        def go(g, wd=wd, dst=dst, half=half, p=p, e=e, f8=f8):
                            return g.dma_start(dst[p][:, half * 4:(half + 1) * 4, :],
                                               wd[e, f8, :, half * 4:(half + 1) * 4, :])
                        return go
                    r.emit("g", wloadh(w1s, w1b, 0),
                           wait=[(pe, pe_w13use[par]), (pe, pe_womm),
                                 (sWC[par], sWC[par].n)],
                           inc=[(sW13h[par], 16)])
                    r.emit("g", wloadh(w3s, w3b, 0), inc=[(sW13h[par], 16)])
                    s13_half = (sW13h[par], sW13h[par].n)
                    r.emit("g", wloadh(w1s, w1b, 1), inc=[(sW13[par], 16)])
                    r.emit("g", wloadh(w3s, w3b, 1), inc=[(sW13[par], 16)])
                    s13_full = sW13[par].n
                    wb13(e, f8)
                for fb in range(4):
                    gpar = fb % 2

                    def hmm(t, wb, P, dbs, fb=fb, p=par, ep=epar):
                        for db in dbs:
                            inst = t.matmul(
                                P[:, 0:NSL], wb[p][:, db, fb * 128:(fb + 1) * 128],
                                xeT[ep][:, db, 0:NSL],
                                start=(db == 0), stop=(db == DB - 1))
                        return inst
                    r.emit("t", lambda t, f=hmm, g=gpar: f(t, w1b, g1p[g], range(4)),
                           wait=[s13_half, (sXT[epar], sXT_cnt[epar]),
                                 (se, se_silu[gpar]), (ve, ve_hmul[gpar]),
                                 (ve, ve_outev[2]), (ve, ve_outev[3])],
                           inc=[(pe, 1)])
                    r.emit("t", lambda t, f=hmm, g=gpar: f(t, w1b, g1p[g], range(4, 8)),
                           wait=[(sW13[par], s13_full)], inc=[(pe, 1)])
                    pe_g1 = pe.n
                    r.emit("t", lambda t, f=hmm, g=gpar: f(t, w3b, g3p[g], range(8)),
                           inc=[(pe, 1)])
                    pe_g3 = pe.n
                    r.emit("s", lambda s, g=gpar: s.activation(
                        sg[g][:, 0:NSL], g1p[g][:, 0:NSL], AF.Silu),
                        wait=[(pe, pe_g1)], inc=[(se, 1)])
                    se_silu[gpar] = se.n
                    fbi = f8 * 4 + fb
                    r.emit("v", lambda v, g=gpar, fbi=fbi: v.tensor_mul(
                        hbuf[:, fbi, :], sg[g][:, 0:NSL], g3p[g][:, 0:NSL]),
                        wait=[(se, se_silu[gpar]), (pe, pe_g3)],
                        inc=[(ve, 1)])
                    ve_hmul[gpar] = ve.n
                pe_w13use[par] = pe.n
            pe_hlast[epar] = pe.n
            ve_hdone = ve.n
            def w2load(q, e=e, w2s=w2s):
                return lambda g, e=e, q=q, w2s=w2s: g.dma_start(
                    w2b[q % 2][:], w2s[e, q])
            if e == 0:
                sW2_cnt = [hoistw2[0], hoistw2[1]]
            else:
                for q in range(2):
                    r.emit("g", w2load(q),
                           wait=[(pe, pe_w2q[q % 2]), (pe, pe_womm),
                                 (sWC2[q % 2], sWC2[q % 2].n)],
                           inc=[(sW2q[q % 2], 16)])
                sW2_cnt = [sW2q[0].n, sW2q[1].n]
                wb2(e, 0)
                wb2(e, 1)

            def ounit(st, nh):
                if st == 0:
                    return PA[:, nh * 512:(nh + 1) * 512]
                if st == 1:
                    return PB[:, nh * 512:(nh + 1) * 512]
                if st == 2:
                    return (PC, PD)[nh][:]
                return (PE2, PF)[nh][:]

            for q in range(4):
                def outmm(t, q=q, sts=None):
                    for st in sts:
                        m = 128 if st < 3 else 64
                        for nh in range(2):
                            for fb in range(8):
                                inst = t.matmul(
                                    ounit(st, nh)[0:m, :],
                                    hbuf[:, q * 8 + fb, st * 128:st * 128 + m],
                                    w2b[q % 2][:, fb, nh * 512:(nh + 1) * 512],
                                    start=(q == 0 and fb == 0),
                                    stop=(q == 3 and fb == 7))
                    return inst
                wts = [(sW2q[q % 2], sW2_cnt[q % 2])]
                if q == 0:
                    wts += [(ve, ve_hdone), (se, se_silu[0]), (se, se_silu[1]),
                            (ve, ve_hmul[0]), (ve, ve_hmul[1]),
                            (ve, ve_outev[0]), (ve, ve_outev[1]),
                            (ve, ve_outev[2]), (ve, ve_outev[3])]
                if q < 3:
                    r.emit("t", lambda t, f=outmm, q=q: f(t, q, (0, 1, 2, 3)),
                           wait=wts, inc=[(pe, 1)])
                else:
                    r.emit("t", lambda t, f=outmm, q=q: f(t, q, (2, 3)),
                           wait=wts, inc=[(pe, 1)])
                    pe_out_a = pe.n
                    r.emit("t", lambda t, f=outmm, q=q: f(t, q, (0, 1)),
                           inc=[(pe, 1)])
                pe_w2q[q % 2] = pe.n
                if q + 2 < 4:
                    r.emit("g", w2load(q + 2),
                           wait=[(pe, pe_w2q[q % 2]),
                                 (sWC2[q % 2], sWC2[q % 2].n)],
                           inc=[(sW2q[q % 2], 16)])
                    sW2_cnt[q % 2] = sW2q[q % 2].n
                    wb2(e, q + 2)
            pe_out = pe.n
            for st in (2, 3, 0, 1):
                spar = st % 2
                m = 128 if st < 3 else 64
                gatep = pe_out_a if st >= 2 else pe_out
                for nh in range(2):
                    r.emit("v", lambda v, st=st, nh=nh, sp=spar, m=m: v.tensor_copy(
                        ostg[sp][0:m, nh * 512:(nh + 1) * 512],
                        ounit(st, nh)[0:m, :]),
                        wait=[(pe, gatep), (sOB[spar], sOB_cnt[spar]),
                              (sGB[spar], sGB_cnt[spar]),
                              (sXO[spar], sXO_cnt[spar])],
                        inc=[(ve, 1)])
                ve_outev[st] = ve.n
                r.emit("y", lambda y, e=e, st=st, sp=spar, m=m: y.dma_start(
                    outhbm[e * SLOT + st * 128:e * SLOT + st * 128 + m, :],
                    ostg[sp][0:m, :]),
                    wait=[(ve, ve_outev[st])], inc=[(sOB[spar], 16)])
                sOB_cnt[spar] = sOB[spar].n
        for tt in range(NW):
            for gi, wsl, gpar in ((g1i, w1sel, 0), (g2i, w2sel, 1)):
                w = [(sOB[0], sOB_cnt[0]), (sOB[1], sOB_cnt[1])] if tt == 0 else []
                r.emit("g", lambda g, gi=gi, tt=tt, p=gpar: g.indirect_dma_start(
                    gstage[p][:], None,
                    outhbm, IndirectOffsetOnAxis(gi[:, tt:tt + 1], 0),
                    oob_is_err=False),
                    wait=w + [(ve, ve_gstt[gpar])], inc=[(sGB[gpar], 16)])
                sGB_cnt[gpar] = sGB[gpar].n
                r.emit("v", lambda v, tt=tt, p=gpar, wsl=wsl: v.scalar_tensor_tensor(
                    acc[:, tt, :], gstage[p][:], wsl[:, tt:tt + 1], acc[:, tt, :],
                    op0=ALU.mult, op1=ALU.add),
                    wait=[(sGB[gpar], sGB_cnt[gpar]), (ve, ve.n)], inc=[(ve, 1)])
                ve_gstt[gpar] = ve.n
        ve_exp = ve.n

        if stage <= 5:
            continue
        for tt in range(NW):
            r.emit("v", lambda v, tt=tt: v.reduce_sum(
                stats[:, 16:17], acc[:, tt, :], axis=AX.X, negate=True),
                wait=[(ve, ve_exp)], inc=[(ve, 1)])
            r.emit("v", lambda v, tt=tt: v.tensor_scalar_mul(
                st2b[:, tt:tt + 1], stats[:, 16:17], 1.0 / D),
                wait=[(ve, ve.n)], inc=[(ve, 1)])
            r.emit("v", lambda v, tt=tt: v.scalar_tensor_tensor(
                x1btm[:, 0, :], acc[:, tt, :], 1.0, acc[:, tt, :],
                op0=ALU.mult, op1=ALU.mult, accum_out=stats[:, 18:19]),
                wait=[(ve, ve.n)], inc=[(ve, 1)])
            r.emit("v", lambda v, tt=tt: v.tensor_mul(
                stats[:, 19:20], st2b[:, tt:tt + 1], st2b[:, tt:tt + 1]),
                wait=[(ve, ve.n)], inc=[(ve, 1)])
            r.emit("v", lambda v, tt=tt: v.tensor_scalar(
                ve8b[:, tt:tt + 1], stats[:, 18:19], 1.0 / D, stats[:, 19:20],
                op0=ALU.mult, op1=ALU.subtract),
                wait=[(ve, ve.n)], inc=[(ve, 1)])
        r.emit("s", lambda s: s.activation(sd8b[:], ve8b[:], AF.Sqrt, bias=cb[:, 0:1]),
               wait=[(ve, ve.n)], inc=[(se, 1)])
        r.emit("v", lambda v: v.reciprocal(rstd8b[:], sd8b[:]),
               wait=[(se, se.n)], inc=[(ve, 1)])
        for tt in range(NW):
            par = tt % 2
            r.emit("v", lambda v, tt=tt, p=par: v.tensor_scalar(
                outf[p][:], acc[:, tt, :], st2b[:, tt:tt + 1],
                rstd8b[:, tt:tt + 1], op0=ALU.add, op1=ALU.mult),
                wait=[(ve, ve.n), (sXO[par], sXO_cnt[par])], inc=[(ve, 1)])
            ve_outf[par] = ve.n
            row0 = c * CH + tt * 128
            r.emit("y", lambda y, r0=row0, p=par: y.dma_start(
                out_d[r0:r0 + 128, :], outf[p][:]),
                wait=[(ve, ve_outf[par])], inc=[(sXO[par], 16)])
            sXO_cnt[par] = sXO[par].n
        pe_endB = pe.n
        ve_endB = ve.n

    dma_sems = [sInit, swA, sMZ, sSC, sW13[0], sW13[1], sW13h[0], sW13h[1],
                sW2q[0], sW2q[1],
                sWC[0], sWC[1], sWC2[0], sWC2[1], sXT[0], sXT[1],
                sXF[0], sXF[1], sOB[0], sOB[1], sGB[0], sGB[1],
                sXO[0], sXO[1]]
    r.emit("y", lambda y: y.nop(),
           wait=[(s, s.n) for s in dma_sems] + [(pe, pe.n), (ve, ve.n), (se, se.n)])

    with nc.Block() as blk:
        def runner(key):
            def go(eng):
                for fn, waits, incs in r.ops[key]:
                    for sem, val in waits:
                        if val > 0:
                            eng.wait_ge(sem.h, val)
                    inst = fn(eng)
                    for sem, k in incs:
                        inst.then_inc(sem.h, k)
            return go
        blk.tensor(runner("t"))
        blk.vector(runner("v"))
        blk.scalar(runner("s"))
        blk.gpsimd(runner("g"))
        blk.sync(runner("y"))
    return nc


def make_consts():
    i = np.arange(W)[:, None]
    j = np.arange(2 * W)[None, :]
    mask = np.where(j - i > W, -3.4028235e38, 0.0).astype(np.float32)
    mask4 = np.ascontiguousarray(np.tile(mask, (1, 4)))
    ident = np.eye(128, dtype=np.float32)
    return mask4, ident


def prep_shared(wq, wkv, wo, bo, gw, w1, w3, w2):
    C = np.ascontiguousarray
    f32 = np.float32
    wq = np.asarray(wq, f32)
    wkv = np.asarray(wkv, f32)
    wo = np.asarray(wo, f32)
    gw = np.asarray(gw, f32)
    w1 = np.asarray(w1, f32)
    w3 = np.asarray(w3, f32)
    w2 = np.asarray(w2, f32)
    mask4, ident = make_consts()
    wqr = C(wq.reshape(DB, 128, D).transpose(1, 0, 2))
    wkvr = C(wkv.reshape(DB, 128, 2 * D).transpose(1, 0, 2))
    gwr = C(gw.reshape(DB, 128, E).transpose(1, 0, 2))
    wor = C(wo.reshape(4, 2, 2, 64, D).transpose(1, 3, 0, 2, 4).reshape(128, DB, D))
    def w13r(w):
        return C(w.reshape(E, DB, 128, 8, 512).transpose(0, 3, 2, 1, 4))
    w2r = C(w2.reshape(E, 4, 8, 128, D).transpose(0, 1, 3, 2, 4))
    bo_bc = C(np.broadcast_to(np.asarray(bo, f32), (128, D)))
    L128 = np.triu(np.ones((128, 128), f32), 1)
    onescol = np.ones((128, 1), f32)
    ones1 = np.ones((1, 128), f32)
    iota8 = C(np.broadcast_to(np.arange(E, dtype=f32), (128, E)))
    return {
        "wq": wqr, "wkv": wkvr, "wo": wor, "gw": gwr,
        "w1": w13r(w1), "w3": w13r(w3), "w2": w2r,
        "bo_bc": bo_bc, "mask4": mask4, "ident": ident,
        "L128": L128, "onescol": onescol, "ones1": ones1,
        "iota8": iota8,
    }


def kernel(L0_attn_x, L0_attn_wq, L0_attn_wkv, L0_attn_wo, L0_attn_bo,
           L0_ffn_gate_w, L0_ffn_w1, L0_ffn_w3, L0_ffn_w2):
    x = np.ascontiguousarray(np.asarray(L0_attn_x, dtype=np.float32))
    Bb, Nn, Dd = x.shape
    n_cores = 8
    TPC = Bb * Nn // n_cores
    n_chunks = TPC // CH
    shared = prep_shared(L0_attn_wq, L0_attn_wkv, L0_attn_wo, L0_attn_bo,
                         L0_ffn_gate_w, L0_ffn_w1, L0_ffn_w3, L0_ffn_w2)
    in_maps = []
    for core in range(n_cores):
        b, s0 = divmod(core * TPC, Nn)
        halo = (np.zeros((W, D), np.float32) if s0 == 0
                else x[b, s0 - W:s0])
        xp = np.ascontiguousarray(np.concatenate([halo, x[b, s0:s0 + TPC]], 0))
        m = dict(shared)
        m["xp"] = xp
        in_maps.append(m)

    nc = bass.Bass("TRN2", target_bir_lowering=False)
    build_core_program(nc, n_chunks)
    res = run_bass_kernel_spmd(nc, in_maps, core_ids=list(range(n_cores)))
    out = np.empty((Bb, Nn, D), np.float32)
    for core in range(n_cores):
        b, s0 = divmod(core * TPC, Nn)
        out[b, s0:s0 + TPC] = res.results[core]["out"]
    return out
